# version 01
# baseline (speedup 1.0000x reference)
import numpy as np
import ml_dtypes
import concourse.bass as bass
import concourse.mybir as mybir
from concourse.bass_utils import run_bass_kernel_spmd
from concourse.alu_op_type import AluOpType as ALU

F32 = mybir.dt.float32
BF16 = mybir.dt.bfloat16
I32 = mybir.dt.int32
AF = mybir.ActivationFunctionType
AX = mybir.AxisListType
BF16NP = ml_dtypes.bfloat16

NCORES = 8

from contextlib import ExitStack

def _sb(nc, st, name, shape, dtype):
    return st.enter_context(nc.sbuf_tensor(name, shape, dtype))

def _ps(nc, st, name, shape, dtype):
    return st.enter_context(nc.psum_tensor(name, shape, dtype))

B, T, D = 4, 4096, 2048
TOK = B * T
TC = TOK // NCORES

ENGINES = ["sync", "vector", "scalar", "tensor", "gpsimd"]

class Prog:
    def __init__(self):
        self.steps = []

    def step(self, eng, fn, inc=1):
        assert eng in ENGINES
        self.steps.append((eng, fn, inc))

    def emit(self, block, sem):
        counts = []
        c = 0
        for (_, _, inc) in self.steps:
            counts.append(c)
            c += inc
        total = c
        def mk(engine_name):
            def body(eng):
                last_wait = -1
                for (e, fn, inc), cbefore in zip(self.steps, counts):
                    if e != engine_name:
                        continue
                    if cbefore > last_wait:
                        eng.wait_ge(sem, cbefore)
                        last_wait = cbefore
                    instr = fn(eng)
                    instr.then_inc(sem, inc)
                if engine_name == "sync":
                    eng.wait_ge(sem, total)
            return body
        for e in ENGINES:
            getattr(block, e)(mk(e))
        return total


def register_consts(nc, cbuf, values):
    steps = []
    for i, v in enumerate(values):
        nc.const_aps.register(F32, float(v), cbuf[:, i:i+1])
        steps.append(("vector", (lambda i=i, v=v: (lambda e: e.memset(cbuf[:, i:i+1], float(v))))(), 1))
    return steps


def mm_group(eng, psum_ap, pairs):
    n = len(pairs)
    instr = None
    for i, (l, r) in enumerate(pairs):
        instr = eng.matmul(psum_ap, l, r, start=(i == 0), stop=(i == n - 1))
    return instr


def build_dequant():
    nc = bass.Bass("TRN2", target_bir_lowering=False)
    wg_pk = nc.dram_tensor("wg_pk", [1024, 2048], I32, kind="ExternalInput").ap()
    wu_pk = nc.dram_tensor("wu_pk", [1024, 2048], I32, kind="ExternalInput").ap()
    wd_pk = nc.dram_tensor("wd_pk", [8192, 256], I32, kind="ExternalInput").ap()
    wg_sc = nc.dram_tensor("wg_sc", [8, 8 * 2048], F32, kind="ExternalInput").ap()
    wu_sc = nc.dram_tensor("wu_sc", [8, 8 * 2048], F32, kind="ExternalInput").ap()
    wd_sc = nc.dram_tensor("wd_sc", [8, 64 * 256], F32, kind="ExternalInput").ap()
    e8 = nc.dram_tensor("e8", [8, 128], F32, kind="ExternalInput").ap()
    g1_w = nc.dram_tensor("g1_w", [2048, 1024], I32, kind="ExternalInput").ap()
    u1_w = nc.dram_tensor("u1_w", [2048, 1024], I32, kind="ExternalInput").ap()
    d1_w = nc.dram_tensor("d1_w", [8192, 256], I32, kind="ExternalInput").ap()
    g1_sc = nc.dram_tensor("g1_sc", [1, 1024], F32, kind="ExternalInput").ap()
    u1_sc = nc.dram_tensor("u1_sc", [1, 1024], F32, kind="ExternalInput").ap()
    d1_sc = nc.dram_tensor("d1_sc", [1, 256], F32, kind="ExternalInput").ap()
    wo0_f = nc.dram_tensor("wo0_f", [2048, 256], F32, kind="ExternalInput").ap()
    wo1_f = nc.dram_tensor("wo1_f", [4096, 256], F32, kind="ExternalInput").ap()
    d_wg = nc.dram_tensor("d_wg", [16, 128, 16, 128], BF16, kind="ExternalOutput").ap()
    d_wu = nc.dram_tensor("d_wu", [16, 128, 16, 128], BF16, kind="ExternalOutput").ap()
    d_wd = nc.dram_tensor("d_wd", [2, 128, 128, 128], BF16, kind="ExternalOutput").ap()
    d_g1 = nc.dram_tensor("d_g1", [8, 128, 16, 128], BF16, kind="ExternalOutput").ap()
    d_u1 = nc.dram_tensor("d_u1", [8, 128, 16, 128], BF16, kind="ExternalOutput").ap()
    d_d1 = nc.dram_tensor("d_d1", [2, 128, 64, 128], BF16, kind="ExternalOutput").ap()
    d_wo0 = nc.dram_tensor("d_wo0", [2048, 256], BF16, kind="ExternalOutput").ap()
    d_wo1 = nc.dram_tensor("d_wo1", [4096, 256], BF16, kind="ExternalOutput").ap()

    with ExitStack() as stx:
        cbuf = _sb(nc, stx, "cbuf", [128, 8], F32)
        pk = _sb(nc, stx, "pk", [128, 2048], I32)
        klo = _sb(nc, stx, "klo", [128, 512], I32)
        khi = _sb(nc, stx, "khi", [128, 512], I32)
        hi_t = _sb(nc, stx, "hi", [128, 512], I32)
        blo = _sb(nc, stx, "blo", [128, 512], I32)
        bhi = _sb(nc, stx, "bhi", [128, 512], I32)
        kf = _sb(nc, stx, "kf", [128, 512], F32)
        r1 = _sb(nc, stx, "r1", [128, 512], F32)
        r2 = _sb(nc, stx, "r2", [128, 512], F32)
        mag = _sb(nc, stx, "mag", [128, 512], F32)
        sgn = _sb(nc, stx, "sgn", [128, 512], F32)
        vlo = _sb(nc, stx, "vlo", [128, 512], BF16)
        vhi = _sb(nc, stx, "vhi", [128, 512], BF16)
        e8sb = _sb(nc, stx, "e8sb", [8, 128], F32)
        scsb = _sb(nc, stx, "scsb", [8, 16384], F32)
        ones1 = _sb(nc, stx, "ones1", [1, 128], F32)
        sc1 = _sb(nc, stx, "sc1", [1, 1024], F32)
        it_t = _sb(nc, stx, "it", [128, 512], I32)
        ft = _sb(nc, stx, "ft", [128, 512], F32)
        bt = _sb(nc, stx, "bt", [128, 512], BF16)
        scps = _ps(nc, stx, "scps", [128, 512], F32)
        bcps = _ps(nc, stx, "bcps", [128, 512], F32)
        sem = stx.enter_context(nc.semaphore("sem"))
        block = stx.enter_context(nc.Block())

        P = Prog()
        for st in register_consts(nc, cbuf, [0.5, -2.0, -6.0, 1.0, -0.25, -128.0]):
            P.step(*st)
        P.step("sync", lambda e: e.dma_start(out=e8sb[:], in_=e8[:]), 16)
        P.step("vector", lambda e: e.memset(ones1[:], 1.0))

        def nv4_matrix(pk_d, sc_d, out_d, KP, NC_, KC, a_scale_rows):
            ntile_a = KP // 128
            P.step("sync", lambda e: e.dma_start(out=scsb[:8, :a_scale_rows * NC_], in_=sc_d[:, :a_scale_rows * NC_]), 16)
            for a in range(ntile_a):
                P.step("sync", (lambda a=a: (lambda e: e.dma_start(out=pk[:, :NC_], in_=pk_d[a*128:(a+1)*128, :])))(), 16)
                for c0 in range(0, NC_, 512):
                    w = min(512, NC_ - c0)
                    pkv = pk[:, c0:c0+w]
                    P.step("tensor", (lambda a=a, c0=c0, w=w: (lambda e: e.matmul(
                        scps[:, :w], e8sb[:, :], scsb[0:8, a*NC_+c0 : a*NC_+c0+w], start=True, stop=True)))())
                    P.step("vector", (lambda pkv=pkv, w=w: (lambda e: e.tensor_scalar(out=hi_t[:, :w], in0=pkv, scalar1=4, scalar2=None, op0=ALU.logical_shift_right)))())
                    P.step("vector", (lambda pkv=pkv, w=w: (lambda e: e.tensor_scalar(out=klo[:, :w], in0=pkv, scalar1=7, scalar2=None, op0=ALU.bitwise_and)))())
                    P.step("vector", (lambda pkv=pkv, w=w: (lambda e: e.tensor_scalar(out=blo[:, :w], in0=pkv, scalar1=8, scalar2=None, op0=ALU.bitwise_and)))())
                    P.step("vector", (lambda w=w: (lambda e: e.tensor_scalar(out=khi[:, :w], in0=hi_t[:, :w], scalar1=7, scalar2=None, op0=ALU.bitwise_and)))())
                    P.step("vector", (lambda w=w: (lambda e: e.tensor_scalar(out=bhi[:, :w], in0=hi_t[:, :w], scalar1=8, scalar2=None, op0=ALU.bitwise_and)))())
                    for (kk, bb, vv, nib) in ((klo, blo, vlo, 0), (khi, bhi, vhi, 1)):
                        P.step("vector", (lambda kk=kk, w=w: (lambda e: e.tensor_copy(out=kf[:, :w], in_=kk[:, :w])))())
                        P.step("scalar", (lambda w=w: (lambda e: e.activation(out=r1[:, :w], in_=kf[:, :w], func=AF.Relu, scale=0.5, bias=-2.0)))())
                        P.step("scalar", (lambda w=w: (lambda e: e.activation(out=r2[:, :w], in_=kf[:, :w], func=AF.Relu, bias=-6.0)))())
                        P.step("vector", (lambda w=w: (lambda e: e.scalar_tensor_tensor(out=mag[:, :w], in0=kf[:, :w], scalar=0.5, in1=r1[:, :w], op0=ALU.mult, op1=ALU.add)))())
                        P.step("vector", (lambda w=w: (lambda e: e.tensor_add(out=mag[:, :w], in0=mag[:, :w], in1=r2[:, :w])))())
                        P.step("scalar", (lambda bb=bb, w=w: (lambda e: e.activation(out=sgn[:, :w], in_=bb[:, :w], func=AF.Copy, scale=-0.25, bias=1.0)))())
                        P.step("vector", (lambda w=w: (lambda e: e.tensor_mul(out=mag[:, :w], in0=mag[:, :w], in1=sgn[:, :w])))())
                        P.step("vector", (lambda vv=vv, w=w: (lambda e: e.tensor_tensor(out=vv[:, :w], in0=mag[:, :w], in1=scps[:, :w], op=ALU.mult)))())
                        nslab = w // 128
                        s0 = c0 // 128
                        src_lo = vv[0:64, :w].rearrange("p (s n) -> p s n", n=128)
                        src_hi = vv[64:128, :w].rearrange("p (s n) -> p s n", n=128)
                        dst_lo = out_d[s0:s0+nslab, nib:128:2, 2*a, :].rearrange("s p n -> p s n")
                        dst_hi = out_d[s0:s0+nslab, nib:128:2, 2*a+1, :].rearrange("s p n -> p s n")
                        P.step("gpsimd", (lambda src=src_lo, dst=dst_lo: (lambda e: e.dma_start(out=dst, in_=src)))(), 16)
                        P.step("gpsimd", (lambda src=src_hi, dst=dst_hi: (lambda e: e.dma_start(out=dst, in_=src)))(), 16)

        nv4_matrix(wg_pk, wg_sc, d_wg, 1024, 2048, 16, 8)
        nv4_matrix(wu_pk, wu_sc, d_wu, 1024, 2048, 16, 8)
        nv4_matrix(wd_pk, wd_sc, d_wd, 8192, 256, 128, 64)

        def int8_matrix(w_d, sc_d, out_d, K, NC_, scn):
            P.step("sync", (lambda: (lambda e: e.dma_start(out=sc1[:, :scn], in_=sc_d[:, :scn])))(), 16)
            for kc in range(K // 128):
                P.step("sync", (lambda kc=kc: (lambda e: e.dma_start(out=pk[:, :NC_], in_=w_d[kc*128:(kc+1)*128, :])))(), 16)
                for c0 in range(0, NC_, 512):
                    w = min(512, NC_ - c0)
                    P.step("tensor", (lambda c0=c0, w=w: (lambda e: e.matmul(
                        bcps[:, :w], ones1[:, :], sc1[:, c0:c0+w], start=True, stop=True)))())
                    P.step("vector", (lambda c0=c0, w=w: (lambda e: e.tensor_copy(out=ft[:, :w], in_=pk[:, c0:c0+w])))())
                    P.step("vector", (lambda w=w: (lambda e: e.scalar_tensor_tensor(out=bt[:, :w], in0=ft[:, :w], scalar=-128.0, in1=bcps[:, :w], op0=ALU.add, op1=ALU.mult)))())
                    nslab = w // 128
                    s0 = c0 // 128
                    src = bt[:, :w].rearrange("p (s n) -> p s n", n=128)
                    dst = out_d[s0:s0+nslab, :, kc, :].rearrange("s p n -> p s n")
                    P.step("gpsimd", (lambda src=src, dst=dst: (lambda e: e.dma_start(out=dst, in_=src)))(), 16)

        int8_matrix(g1_w, g1_sc, d_g1, 2048, 1024, 1024)
        int8_matrix(u1_w, u1_sc, d_u1, 2048, 1024, 1024)
        int8_matrix(d1_w, d1_sc, d_d1, 8192, 256, 256)

        def cast_matrix(w_d, out_d, K, NC_):
            for kc in range(K // 128):
                P.step("sync", (lambda kc=kc: (lambda e: e.dma_start(out=ft[:, :NC_], in_=w_d[kc*128:(kc+1)*128, :])))(), 16)
                P.step("vector", (lambda: (lambda e: e.tensor_copy(out=bt[:, :NC_], in_=ft[:, :NC_])))())
                P.step("gpsimd", (lambda kc=kc: (lambda e: e.dma_start(out=out_d[kc*128:(kc+1)*128, :], in_=bt[:, :NC_])))(), 16)

        cast_matrix(wo0_f, d_wo0, 2048, 256)
        cast_matrix(wo1_f, d_wo1, 4096, 256)

        P.emit(block, sem)
    return nc


def dequant_inputs(core, I):
    tp = lambda x: np.ascontiguousarray(np.asarray(x).T)
    c = core
    wgT = tp(I["L0_ffn_gate_W_packed"])
    wuT = tp(I["L0_ffn_up_W_packed"])
    wdT = tp(I["L0_ffn_down_W_packed"])
    def screg(sc, a_rows):
        scT = tp(sc)
        KB, N = scT.shape
        assert KB == 8 * a_rows
        out = scT.reshape(a_rows, 8, N).transpose(1, 0, 2).reshape(8, a_rows * N)
        return np.ascontiguousarray(out)
    gsc = screg(I["L0_ffn_gate_scales"], 8)
    usc = screg(I["L0_ffn_up_scales"], 8)
    dsc = screg(I["L0_ffn_down_scales"], 64)
    e8 = np.zeros((8, 128), dtype=np.float32)
    for r in range(8):
        e8[r, r*16:(r+1)*16] = 1.0
    g1T = tp(I["L1_ffn_gate_w"])
    u1T = tp(I["L1_ffn_up_w"])
    d1T = tp(I["L1_ffn_down_w"])
    ns = slice(c*2048, (c+1)*2048)
    hs = slice(c*256, (c+1)*256)
    i1s = slice(c*1024, (c+1)*1024)
    return {
        "wg_pk": np.ascontiguousarray(wgT[:, ns]),
        "wu_pk": np.ascontiguousarray(wuT[:, ns]),
        "wd_pk": np.ascontiguousarray(wdT[:, hs]),
        "wg_sc": np.ascontiguousarray(gsc.reshape(8, 8, 16384)[:, :, ns].reshape(8, 8*2048)),
        "wu_sc": np.ascontiguousarray(usc.reshape(8, 8, 16384)[:, :, ns].reshape(8, 8*2048)),
        "wd_sc": np.ascontiguousarray(dsc.reshape(8, 64, 2048)[:, :, hs].reshape(8, 64*256)),
        "e8": e8,
        "g1_w": np.ascontiguousarray(g1T[:, i1s]).astype(np.int32),
        "u1_w": np.ascontiguousarray(u1T[:, i1s]).astype(np.int32),
        "d1_w": np.ascontiguousarray(d1T[:, hs]).astype(np.int32),
        "g1_sc": np.asarray(I["L1_ffn_gate_scale"])[None, i1s].astype(np.float32),
        "u1_sc": np.asarray(I["L1_ffn_up_scale"])[None, i1s].astype(np.float32),
        "d1_sc": np.asarray(I["L1_ffn_down_scale"])[None, hs].astype(np.float32),
        "wo0_f": np.ascontiguousarray(np.asarray(I["L0_attn_Wo"])[:, hs]).astype(np.float32),
        "wo1_f": np.ascontiguousarray(np.asarray(I["L1_attn_Wo"])[:, hs]).astype(np.float32),
    }


def assemble_dequant(res):
    cat = lambda key, axis: np.concatenate([res[c][key] for c in range(NCORES)], axis=axis)
    return {
        "wg": cat("d_wg", 0),
        "wu": cat("d_wu", 0),
        "wd": cat("d_wd", 0),
        "g1": cat("d_g1", 0),
        "u1": cat("d_u1", 0),
        "dn1": cat("d_d1", 0),
        "wo0": cat("d_wo0", 1),
        "wo1": cat("d_wo1", 1),
    }


def build_ffn(kind, prologue_od):
    L0 = (kind == "L0")
    I = 16384 if L0 else 8192
    NSTRIP = I // 2048
    nc = bass.Bass("TRN2", target_bir_lowering=False)
    x_in = nc.dram_tensor("x_in", [TC, D], F32, kind="ExternalInput").ap()
    ident_in = nc.dram_tensor("ident_in", [128, 128], BF16, kind="ExternalInput").ap()
    identf_in = nc.dram_tensor("identf_in", [128, 128], F32, kind="ExternalInput").ap()
    wgb = nc.dram_tensor("wgb", [I // 128, 128, 16, 128], BF16, kind="ExternalInput").ap()
    wub = nc.dram_tensor("wub", [I // 128, 128, 16, 128], BF16, kind="ExternalInput").ap()
    wdb = nc.dram_tensor("wdb", [16, 128, I // 128, 128], BF16, kind="ExternalInput").ap()
    x_out = nc.dram_tensor("x_out", [TC, D], F32, kind="ExternalOutput").ap()
    if prologue_od:
        OD = prologue_od
        oT_in = nc.dram_tensor("oT_in", [OD, TC], BF16, kind="ExternalInput").ap()
        wo_in = nc.dram_tensor("wo_in", [OD, D], BF16, kind="ExternalInput").ap()
        xprime = nc.dram_tensor("xprime", [TC, D], F32, kind="Internal").ap()
        xbase = xprime
    else:
        OD = 0
        xbase = x_in

    with ExitStack() as stx:
        cbuf = _sb(nc, stx, "cbuf", [128, 8], F32)
        yT = _sb(nc, stx, "yT", [128, 16 * 1024], BF16)
        inter = _sb(nc, stx, "inter", [128, 16 * 1024], BF16)
        acc = _sb(nc, stx, "acc", [128, 16 * 1024], F32)
        wsl = _sb(nc, stx, "wsl", [128, 2048], BF16)
        wsl2 = _sb(nc, stx, "wsl2", [128, 2048], BF16)
        wdl = _sb(nc, stx, "wdl", [128, 2048], BF16)
        xt = _sb(nc, stx, "xt", [128, 2048], F32)
        xc = _sb(nc, stx, "xc", [128, 2048], F32)
        xb = _sb(nc, stx, "xb", [128, 2048], BF16)
        sq = _sb(nc, stx, "sq", [128, 2048], F32)
        st = _sb(nc, stx, "st", [128, 4], F32)
        identb = _sb(nc, stx, "ident", [128, 128], BF16)
        identf = _sb(nc, stx, "identf", [128, 128], F32)
        g1t = _sb(nc, stx, "g1t", [128, 512], F32)
        g2t = _sb(nc, stx, "g2t", [128, 512], F32)
        g3t = _sb(nc, stx, "g3t", [128, 512], F32)
        outt = _sb(nc, stx, "outt", [128, 2048], F32)
        wot = _sb(nc, stx, "wot", [128, 512], BF16)
        otb = _sb(nc, stx, "otb", [128, 512], BF16)
        psg = _ps(nc, stx, "psg", [128, 512], F32)
        psu = _ps(nc, stx, "psu", [128, 512], F32)
        psd = _ps(nc, stx, "psd", [128, 512], F32)
        pstb = _ps(nc, stx, "pst", [128, 512], BF16)
        pstf = _ps(nc, stx, "pstf", [128, 512], F32)
        pswo = _ps(nc, stx, "pswo", [128, 512], F32)
        sem = stx.enter_context(nc.semaphore("sem"))
        block = stx.enter_context(nc.Block())

        P = Prog()
        for s in register_consts(nc, cbuf, [0.044715, 0.7978845608028654, 1e-5, 1.0/D, 0.5]):
            P.step(*s)
        P.step("sync", lambda e: e.dma_start(out=identb[:], in_=ident_in[:]), 16)
        P.step("sync", lambda e: e.dma_start(out=identf[:], in_=identf_in[:]), 16)

        NODC = OD // 128 if OD else 0

        def prologue():
            psbanks = [psg, psu, psd, pstf, pswo, pstb]
            banks = [psg, psu, psd, pswo]
            for fc in range(4):
                for ttg in range(4):
                    tts = [ttg * 4 + k for k in range(4)]
                    for odc in range(NODC):
                        P.step("sync", (lambda odc=odc, fc=fc: (lambda e: e.dma_start(
                            out=wot[:, 0:512], in_=wo_in[odc*128:(odc+1)*128, fc*512:(fc+1)*512])))(), 16)
                        for k, tt in enumerate(tts):
                            P.step("sync", (lambda odc=odc, tt=tt, k=k: (lambda e: e.dma_start(
                                out=otb[:, k*128:(k+1)*128], in_=oT_in[odc*128:(odc+1)*128, tt*128:(tt+1)*128])))(), 16)
                        for k in range(4):
                            P.step("tensor", (lambda k=k, odc=odc: (lambda e: e.matmul(
                                banks[k][:, :], otb[:, k*128:(k+1)*128], wot[:, 0:512],
                                start=(odc == 0), stop=(odc == NODC - 1))))())
                    for k, tt in enumerate(tts):
                        P.step("sync", (lambda tt=tt, fc=fc: (lambda e: e.dma_start(
                            out=xt[:, 0:512], in_=x_in[tt*128:(tt+1)*128, fc*512:(fc+1)*512])))(), 16)
                        P.step("vector", (lambda k=k: (lambda e: e.scalar_tensor_tensor(
                            out=outt[:, 0:512], in0=banks[k][:, :], scalar=1.0, in1=xt[:, 0:512],
                            op0=ALU.mult, op1=ALU.add)))())
                        P.step("gpsimd", (lambda tt=tt, fc=fc: (lambda e: e.dma_start(
                            out=xprime[tt*128:(tt+1)*128, fc*512:(fc+1)*512], in_=outt[:, 0:512])))(), 16)

        if prologue_od:
            prologue()

        def ln_tile(tp, tt):
            r0 = tp * 1024 + tt * 128
            P.step("sync", (lambda: (lambda e: e.dma_start(out=xt[:], in_=xbase[r0:r0+128, :])))(), 16)
            P.step("vector", lambda e: e.reduce_sum(out=st[:, 0:1], in_=xt[:], axis=AX.X))
            P.step("vector", lambda e: e.tensor_scalar(out=st[:, 0:1], in0=st[:, 0:1], scalar1=1.0/D, scalar2=None, op0=ALU.mult))
            P.step("vector", lambda e: e.tensor_scalar(out=xc[:], in0=xt[:], scalar1=st[:, 0:1], scalar2=None, op0=ALU.subtract))
            P.step("scalar", lambda e: e.activation(out=sq[:], in_=xc[:], func=AF.Square))
            P.step("vector", lambda e: e.reduce_sum(out=st[:, 1:2], in_=sq[:], axis=AX.X))
            P.step("vector", lambda e: e.tensor_scalar(out=st[:, 1:2], in0=st[:, 1:2], scalar1=1.0/D, scalar2=1e-5, op0=ALU.mult, op1=ALU.add))
            P.step("vector", lambda e: e.reciprocal(out=st[:, 2:3], in_=st[:, 1:2]))
            P.step("scalar", lambda e: e.activation(out=st[:, 3:4], in_=st[:, 2:3], func=AF.Sqrt))
            P.step("vector", lambda e: e.tensor_scalar(out=xb[:], in0=xc[:], scalar1=st[:, 3:4], scalar2=None, op0=ALU.mult))
            for gp in range(4):
                for k in range(4):
                    fc = gp * 4 + k
                    P.step("tensor", (lambda fc=fc, k=k: (lambda e: e.transpose(
                        pstb[:, k*128:(k+1)*128], xb[:, fc*128:(fc+1)*128], identb[:])))())
                for k in range(4):
                    fc = gp * 4 + k
                    P.step("vector", (lambda fc=fc, k=k, tt=tt: (lambda e: e.tensor_copy(
                        out=yT[:, fc*1024 + tt*128 : fc*1024 + (tt+1)*128], in_=pstb[:, k*128:(k+1)*128])))())

        def gelu_combine(w):
            P.step("scalar", lambda e: e.activation(out=g1t[:, :w], in_=psg[:, :w], func=AF.Square))
            P.step("vector", lambda e: e.tensor_scalar(out=g1t[:, :w], in0=g1t[:, :w], scalar1=0.044715, scalar2=0.7978845608028654, op0=ALU.mult, op1=ALU.add))
            P.step("vector", lambda e: e.tensor_tensor(out=g2t[:, :w], in0=g1t[:, :w], in1=psg[:, :w], op=ALU.mult))
            P.step("scalar", lambda e: e.activation(out=g2t[:, :w], in_=g2t[:, :w], func=AF.Tanh))
            P.step("vector", lambda e: e.scalar_tensor_tensor(out=g3t[:, :w], in0=g2t[:, :w], scalar=1.0, in1=psg[:, :w], op0=ALU.add, op1=ALU.mult))
            P.step("vector", lambda e: e.scalar_tensor_tensor(out=g3t[:, :w], in0=g3t[:, :w], scalar=0.5, in1=psu[:, :w], op0=ALU.mult, op1=ALU.mult))

        def silu_combine(w):
            P.step("scalar", lambda e: e.activation(out=g1t[:, :w], in_=psg[:, :w], func=AF.Silu))
            P.step("vector", lambda e: e.tensor_tensor(out=g3t[:, :w], in0=g1t[:, :w], in1=psu[:, :w], op=ALU.mult))

        for tp in range(2):
            for tt in range(8):
                ln_tile(tp, tt)
            for s in range(NSTRIP):
                for nt in range(16):
                    slab = s * 16 + nt
                    P.step("sync", (lambda slab=slab: (lambda e: e.dma_start(
                        out=wsl[:], in_=wgb[slab].rearrange("p kc n -> p (kc n)"))))(), 16)
                    P.step("sync", (lambda slab=slab: (lambda e: e.dma_start(
                        out=wsl2[:], in_=wub[slab].rearrange("p kc n -> p (kc n)"))))(), 16)
                    for tc2 in range(2):
                        tcs = tc2 * 512
                        P.step("tensor", (lambda tcs=tcs: (lambda e: mm_group(e, psg[:, :],
                            [(wsl[:, kc*128:(kc+1)*128], yT[:, kc*1024 + tcs : kc*1024 + tcs + 512]) for kc in range(16)])))())
                        P.step("tensor", (lambda tcs=tcs: (lambda e: mm_group(e, psu[:, :],
                            [(wsl2[:, kc*128:(kc+1)*128], yT[:, kc*1024 + tcs : kc*1024 + tcs + 512]) for kc in range(16)])))())
                        if L0:
                            gelu_combine(512)
                        else:
                            silu_combine(512)
                        P.step("vector", (lambda nt=nt, tcs=tcs: (lambda e: e.tensor_copy(
                            out=inter[:, nt*1024 + tcs : nt*1024 + tcs + 512], in_=g3t[:, 0:512])))())
                for ht in range(16):
                    P.step("sync", (lambda ht=ht, s=s: (lambda e: e.dma_start(
                        out=wdl[:], in_=wdb[ht, :, s*16:(s+1)*16, :].rearrange("p kc n -> p (kc n)"))))(), 16)
                    for tc2 in range(2):
                        tcs = tc2 * 512
                        P.step("tensor", (lambda tcs=tcs: (lambda e: mm_group(e, psd[:, :],
                            [(wdl[:, j*128:(j+1)*128], inter[:, j*1024 + tcs : j*1024 + tcs + 512]) for j in range(16)])))())
                        if s == 0:
                            P.step("vector", (lambda ht=ht, tcs=tcs: (lambda e: e.tensor_copy(
                                out=acc[:, ht*1024 + tcs : ht*1024 + tcs + 512], in_=psd[:, :])))())
                        else:
                            P.step("vector", (lambda ht=ht, tcs=tcs: (lambda e: e.tensor_add(
                                out=acc[:, ht*1024 + tcs : ht*1024 + tcs + 512],
                                in0=acc[:, ht*1024 + tcs : ht*1024 + tcs + 512], in1=psd[:, :])))())
            for tt in range(8):
                r0 = tp * 1024 + tt * 128
                P.step("sync", (lambda r0=r0: (lambda e: e.dma_start(out=xt[:], in_=xbase[r0:r0+128, :])))(), 16)
                for gp in range(4):
                    for k in range(4):
                        ht = gp * 4 + k
                        P.step("tensor", (lambda ht=ht, k=k, tt=tt: (lambda e: e.transpose(
                            pstf[:, k*128:(k+1)*128], acc[:, ht*1024 + tt*128 : ht*1024 + (tt+1)*128], identf[:])))())
                    P.step("vector", (lambda gp=gp: (lambda e: e.scalar_tensor_tensor(
                        out=outt[:, gp*512:(gp+1)*512], in0=pstf[:, :], scalar=0.5,
                        in1=xt[:, gp*512:(gp+1)*512], op0=ALU.mult, op1=ALU.add)))())
                P.step("gpsimd", (lambda r0=r0: (lambda e: e.dma_start(out=x_out[r0:r0+128, :], in_=outt[:])))(), 16)

        P.emit(block, sem)
    return nc


SCALE1 = 1.0 / float(np.sqrt(128.0))
CSHIFT = 30.0

def build_attn1():
    nc = bass.Bass("TRN2", target_bir_lowering=False)
    x_in = nc.dram_tensor("x_in", [T, D], F32, kind="ExternalInput").ap()
    wq_in = nc.dram_tensor("wq_in", [D, 2048], F32, kind="ExternalInput").ap()
    wk_in = nc.dram_tensor("wk_in", [D, 512], F32, kind="ExternalInput").ap()
    wv_in = nc.dram_tensor("wv_in", [D, 512], F32, kind="ExternalInput").ap()
    cos_in = nc.dram_tensor("cos_in", [128, 32 * 64], F32, kind="ExternalInput").ap()
    sin_in = nc.dram_tensor("sin_in", [128, 32 * 64], F32, kind="ExternalInput").ap()
    lamv = nc.dram_tensor("lamv", [1, 512], F32, kind="ExternalInput").ap()
    gnw_in = nc.dram_tensor("gnw_in", [128, 16], F32, kind="ExternalInput").ap()
    gnb_in = nc.dram_tensor("gnb_in", [128, 16], F32, kind="ExternalInput").ap()
    dmask_in = nc.dram_tensor("dmask_in", [128, 4 * 512], F32, kind="ExternalInput").ap()
    ident_in = nc.dram_tensor("ident_in", [128, 128], BF16, kind="ExternalInput").ap()
    oT_out = nc.dram_tensor("oT_out", [2048, T], BF16, kind="ExternalOutput").ap()
    xhT = nc.dram_tensor("xhT", [D, T], BF16, kind="Internal").ap()

    with ExitStack() as stx:
        cbuf = _sb(nc, stx, "cbuf", [128, 8], F32)
        xt = _sb(nc, stx, "xt", [128, 2048], F32)
        xc = _sb(nc, stx, "xc", [128, 2048], F32)
        sq = _sb(nc, stx, "sq", [128, 2048], F32)
        st = _sb(nc, stx, "st", [128, 8], F32)
        xb = _sb(nc, stx, "xb", [128, 2048], BF16)
        xq = _sb(nc, stx, "xq", [128, 2048], BF16)
        identb = _sb(nc, stx, "ident", [128, 128], BF16)
        cossb = _sb(nc, stx, "cossb", [128, 2048], F32)
        sinsb = _sb(nc, stx, "sinsb", [128, 2048], F32)
        wkb = _sb(nc, stx, "wkb", [128, 16 * 512], BF16)
        wvb = _sb(nc, stx, "wvb", [128, 16 * 512], BF16)
        wqb = _sb(nc, stx, "wqb", [128, 16 * 256], BF16)
        KT = _sb(nc, stx, "KT", [128, 4 * T], BF16)
        Vb = _sb(nc, stx, "Vb", [128, 32 * 512], BF16)
        QT = wkb
        E1 = _sb(nc, stx, "E1", [128, T], BF16)
        E2 = _sb(nc, stx, "E2", [128, T], BF16)
        Pb = _sb(nc, stx, "Pb", [128, T], BF16)
        PT = _sb(nc, stx, "PT", [128, T], BF16)
        oh = wvb
        dmsk = _sb(nc, stx, "dmsk", [128, 4 * 512], F32)
        ktmp = _sb(nc, stx, "ktmp", [128, 512], F32)
        ktb = _sb(nc, stx, "ktb", [128, 512], BF16)
        sct = _sb(nc, stx, "sct", [128, 512], F32)
        gnw = _sb(nc, stx, "gnw", [128, 16], F32)
        gnb = _sb(nc, stx, "gnb", [128, 16], F32)
        lam_s = _sb(nc, stx, "lam_s", [128, 16], F32)
        ones1 = _sb(nc, stx, "ones1", [1, 128], F32)
        onescol = _sb(nc, stx, "onescol", [128, 1], F32)
        gst = _sb(nc, stx, "gst", [128, 8], F32)
        sc1 = _sb(nc, stx, "sc1", [1, 8], F32)
        psA = _ps(nc, stx, "psA", [128, 512], F32)
        psB = _ps(nc, stx, "psB", [128, 512], F32)
        psT = _ps(nc, stx, "psT", [128, 512], BF16)
        psO1 = _ps(nc, stx, "psO1", [128, 512], F32)
        psO2 = _ps(nc, stx, "psO2", [128, 512], F32)
        psS = _ps(nc, stx, "psS", [128, 128], F32)
        sem = stx.enter_context(nc.semaphore("sem"))
        block = stx.enter_context(nc.Block())

        P = Prog()
        for s in register_consts(nc, cbuf, [SCALE1, -CSHIFT, 1e-5, 1.0, -1.0, 0.8, 1.0 / (T * 256.0)]):
            P.step(*s)
        P.step("sync", lambda e: e.dma_start(out=identb[:], in_=ident_in[:]), 16)
        P.step("sync", lambda e: e.dma_start(out=cossb[:], in_=cos_in[:]), 16)
        P.step("sync", lambda e: e.dma_start(out=sinsb[:], in_=sin_in[:]), 16)
        P.step("sync", lambda e: e.dma_start(out=dmsk[:], in_=dmask_in[:]), 16)
        P.step("sync", lambda e: e.dma_start(out=gnw[:, :], in_=gnw_in[:]), 16)
        P.step("sync", lambda e: e.dma_start(out=gnb[:, :], in_=gnb_in[:]), 16)
        P.step("vector", lambda e: e.memset(ones1[:], 1.0))
        P.step("vector", lambda e: e.memset(onescol[:], 1.0))
        P.step("vector", lambda e: e.memset(gst[:], 0.0))

        def lam_steps():
            P.step("sync", lambda e: e.dma_start(out=xt[0:1, 0:512], in_=lamv[:, :]), 16)
            P.step("vector", lambda e: e.tensor_mul(out=xc[0:1, 0:128], in0=xt[0:1, 0:128], in1=xt[0:1, 128:256]))
            P.step("vector", lambda e: e.tensor_mul(out=xc[0:1, 128:256], in0=xt[0:1, 256:384], in1=xt[0:1, 384:512]))
            P.step("vector", lambda e: e.reduce_sum(out=sc1[0:1, 0:1], in_=xc[0:1, 0:128], axis=AX.X))
            P.step("vector", lambda e: e.reduce_sum(out=sc1[0:1, 1:2], in_=xc[0:1, 128:256], axis=AX.X))
            P.step("scalar", lambda e: e.activation(out=sc1[0:1, 2:3], in_=sc1[0:1, 0:1], func=AF.Exp))
            P.step("scalar", lambda e: e.activation(out=sc1[0:1, 3:4], in_=sc1[0:1, 1:2], func=AF.Exp))
            P.step("vector", lambda e: e.tensor_sub(out=sc1[0:1, 4:5], in0=sc1[0:1, 3:4], in1=sc1[0:1, 2:3]))
            P.step("vector", lambda e: e.tensor_scalar(out=sc1[0:1, 4:5], in0=sc1[0:1, 4:5], scalar1=1.0, scalar2=-0.2, op0=ALU.mult, op1=ALU.add))
            P.step("tensor", lambda e: e.matmul(psS[:, 0:1], ones1[:, :], sc1[0:1, 4:5], start=True, stop=True))
            P.step("vector", lambda e: e.tensor_copy(out=lam_s[:, 8:9], in_=psS[:, 0:1]))
        lam_steps()

        def loadcast(w_in, dst, cols):
            for fc in range(16):
                P.step("sync", (lambda fc=fc: (lambda e: e.dma_start(out=xt[:, 0:cols], in_=w_in[fc*128:(fc+1)*128, :])))(), 16)
                P.step("vector", (lambda fc=fc: (lambda e: e.tensor_copy(out=dst[:, fc*cols:(fc+1)*cols], in_=xt[:, 0:cols])))())
        loadcast(wk_in, wkb, 512)
        loadcast(wv_in, wvb, 512)

        def rope_block(src_ap, dst_ap, tt, tmp):
            cs = cossb[:, tt*64:(tt+1)*64]
            sn = sinsb[:, tt*64:(tt+1)*64]
            x1 = src_ap[:, 0:128:2]; x2 = src_ap[:, 1:128:2]
            t1 = tmp[:, 0:64]; t2 = tmp[:, 64:128]
            P.step("vector", lambda e: e.tensor_tensor(out=t1, in0=x1, in1=cs, op=ALU.mult))
            P.step("vector", lambda e: e.scalar_tensor_tensor(out=t2, in0=x2, scalar=-1.0, in1=sn, op0=ALU.mult, op1=ALU.mult))
            P.step("vector", lambda e: e.tensor_add(out=dst_ap[:, 0:128:2], in0=t1, in1=t2))
            P.step("vector", lambda e: e.tensor_tensor(out=t1, in0=x1, in1=sn, op=ALU.mult))
            P.step("vector", lambda e: e.tensor_tensor(out=t2, in0=x2, in1=cs, op=ALU.mult))
            P.step("vector", lambda e: e.tensor_add(out=dst_ap[:, 1:128:2], in0=t1, in1=t2))

        for tt in range(32):
            r0 = tt * 128
            P.step("sync", (lambda r0=r0: (lambda e: e.dma_start(out=xt[:], in_=x_in[r0:r0+128, :])))(), 16)
            P.step("vector", lambda e: e.reduce_sum(out=st[:, 0:1], in_=xt[:], axis=AX.X))
            P.step("vector", lambda e: e.tensor_scalar(out=st[:, 0:1], in0=st[:, 0:1], scalar1=1.0/D, scalar2=None, op0=ALU.mult))
            P.step("vector", lambda e: e.tensor_scalar(out=xc[:], in0=xt[:], scalar1=st[:, 0:1], scalar2=None, op0=ALU.subtract))
            P.step("scalar", lambda e: e.activation(out=sq[:], in_=xc[:], func=AF.Square))
            P.step("vector", lambda e: e.reduce_sum(out=st[:, 1:2], in_=sq[:], axis=AX.X))
            P.step("vector", lambda e: e.tensor_scalar(out=st[:, 1:2], in0=st[:, 1:2], scalar1=1.0/D, scalar2=1e-5, op0=ALU.mult, op1=ALU.add))
            P.step("vector", lambda e: e.reciprocal(out=st[:, 2:3], in_=st[:, 1:2]))
            P.step("scalar", lambda e: e.activation(out=st[:, 3:4], in_=st[:, 2:3], func=AF.Sqrt))
            P.step("vector", lambda e: e.tensor_scalar(out=xb[:], in0=xc[:], scalar1=st[:, 3:4], scalar2=None, op0=ALU.mult))
            for gp in range(4):
                for k in range(4):
                    fc = gp * 4 + k
                    P.step("tensor", (lambda fc=fc, k=k: (lambda e: e.transpose(
                        psT[:, k*128:(k+1)*128], xb[:, fc*128:(fc+1)*128], identb[:])))())
                for k in range(4):
                    fc = gp * 4 + k
                    P.step("vector", (lambda fc=fc, k=k: (lambda e: e.tensor_copy(
                        out=xq[:, fc*128:(fc+1)*128], in_=psT[:, k*128:(k+1)*128])))())
            P.step("gpsimd", (lambda tt=tt: (lambda e: e.dma_start(
                out=xhT.rearrange("(fc p) t -> p fc t", p=128)[:, :, tt*128:(tt+1)*128],
                in_=xq[:].rearrange("p (fc t) -> p fc t", t=128))))(), 16)
            P.step("tensor", lambda e: mm_group(e, psA[:, :],
                [(xq[:, fc*128:(fc+1)*128], wkb[:, fc*512:(fc+1)*512]) for fc in range(16)]))
            P.step("tensor", lambda e: mm_group(e, psB[:, :],
                [(xq[:, fc*128:(fc+1)*128], wvb[:, fc*512:(fc+1)*512]) for fc in range(16)]))
            for blk in range(4):
                rope_block(psA[:, blk*128:(blk+1)*128], ktb[:, blk*128:(blk+1)*128], tt, ktmp)
            for blk in range(4):
                P.step("tensor", (lambda blk=blk: (lambda e: e.transpose(
                    psT[:, blk*128:(blk+1)*128], ktb[:, blk*128:(blk+1)*128], identb[:])))())
            for blk in range(4):
                P.step("vector", (lambda blk=blk, tt=tt: (lambda e: e.tensor_copy(
                    out=KT[:, blk*T + tt*128 : blk*T + (tt+1)*128], in_=psT[:, blk*128:(blk+1)*128])))())
            P.step("vector", (lambda tt=tt: (lambda e: e.tensor_copy(out=Vb[:, tt*512:(tt+1)*512], in_=psB[:, :])))())

        for hh in range(8):
            kv = hh // 4
            loadcast(wq_in[:, hh*256:(hh+1)*256], wqb, 256)
            for tt in range(32):
                P.step("sync", (lambda tt=tt: (lambda e: e.dma_start(
                    out=xq[:].rearrange("p (fc t) -> p fc t", t=128),
                    in_=xhT.rearrange("(fc p) t -> p fc t", p=128)[:, :, tt*128:(tt+1)*128])))(), 16)
                P.step("tensor", lambda e: mm_group(e, psA[:, 0:256],
                    [(xq[:, fc*128:(fc+1)*128], wqb[:, fc*256:(fc+1)*256]) for fc in range(16)]))
                for blk in range(2):
                    rope_block(psA[:, blk*128:(blk+1)*128], ktb[:, blk*128:(blk+1)*128], tt, ktmp)
                for blk in range(2):
                    P.step("tensor", (lambda blk=blk: (lambda e: e.transpose(
                        psT[:, blk*128:(blk+1)*128], ktb[:, blk*128:(blk+1)*128], identb[:])))())
                for blk in range(2):
                    P.step("vector", (lambda blk=blk, tt=tt: (lambda e: e.tensor_copy(
                        out=QT[:, blk*T + tt*128 : blk*T + (tt+1)*128], in_=psT[:, blk*128:(blk+1)*128])))())
            for tt in range(32):
                nsc = tt // 4 + 1
                for half, Ebuf, kbi in ((0, E1, 2*kv), (1, E2, 2*kv+1)):
                    for sc in range(nsc):
                        diag = (sc == nsc - 1)
                        P.step("tensor", (lambda half=half, kbi=kbi, tt=tt, sc=sc: (lambda e: e.matmul(
                            psA[:, :], QT[:, half*T + tt*128 : half*T + (tt+1)*128],
                            KT[:, kbi*T + sc*512 : kbi*T + (sc+1)*512], start=True, stop=True)))())
                        if diag:
                            P.step("vector", (lambda tt=tt: (lambda e: e.tensor_add(
                                out=sct[:, :], in0=psA[:, :], in1=dmsk[:, (tt % 4)*512:(tt % 4 + 1)*512])))())
                            src = sct
                        else:
                            src = psA
                        P.step("scalar", (lambda src=src, Ebuf=Ebuf, sc=sc, half=half: (lambda e: e.activation(
                            out=Ebuf[:, sc*512:(sc+1)*512], in_=src[:, :], func=AF.Exp,
                            scale=SCALE1, bias=-CSHIFT,
                            accum_out=st[:, 4+half:5+half] if sc == 0 else st[:, 6:7])))())
                        if sc != 0:
                            P.step("vector", (lambda half=half: (lambda e: e.tensor_add(
                                out=st[:, 4+half:5+half], in0=st[:, 4+half:5+half], in1=st[:, 6:7])))())
                P.step("vector", lambda e: e.reciprocal(out=st[:, 4:5], in_=st[:, 4:5]))
                P.step("vector", lambda e: e.reciprocal(out=st[:, 5:6], in_=st[:, 5:6]))
                P.step("vector", lambda e: e.tensor_scalar(out=st[:, 6:7], in0=st[:, 5:6], scalar1=lam_s[:, 8:9], scalar2=None, op0=ALU.mult))
                sw = nsc * 512
                P.step("vector", (lambda sw=sw: (lambda e: e.tensor_scalar(out=Pb[:, 0:sw], in0=E1[:, 0:sw], scalar1=st[:, 4:5], scalar2=None, op0=ALU.mult)))())
                P.step("vector", (lambda sw=sw: (lambda e: e.scalar_tensor_tensor(out=Pb[:, 0:sw], in0=E2[:, 0:sw], scalar=st[:, 6:7], in1=Pb[:, 0:sw], op0=ALU.mult, op1=ALU.add)))())
                nst = nsc * 4
                for g0 in range(0, nst, 4):
                    kmax = min(4, nst - g0)
                    for k in range(kmax):
                        P.step("tensor", (lambda g0=g0, k=k: (lambda e: e.transpose(
                            psT[:, k*128:(k+1)*128], Pb[:, (g0+k)*128:(g0+k+1)*128], identb[:])))())
                    for k in range(kmax):
                        P.step("vector", (lambda g0=g0, k=k: (lambda e: e.tensor_copy(
                            out=PT[:, (g0+k)*128:(g0+k+1)*128], in_=psT[:, k*128:(k+1)*128])))())
                for dc, psO in ((0, psO1), (1, psO2)):
                    P.step("tensor", (lambda dc=dc, psO=psO, nst=nst, tt=tt: (lambda e: mm_group(e, psO[:, 0:128],
                        [(Vb[:, stl*512 + dc*128 : stl*512 + (dc+1)*128], PT[:, stl*128:(stl+1)*128])
                         for stl in range(nst)])))())
                    P.step("vector", (lambda psO=psO: (lambda e: e.reduce_sum(out=st[:, 6:7], in_=psO[:, 0:128], axis=AX.X)))())
                    P.step("vector", lambda e: e.tensor_add(out=gst[:, 0:1], in0=gst[:, 0:1], in1=st[:, 6:7]))
                    P.step("scalar", (lambda psO=psO: (lambda e: e.activation(out=ktmp[:, 0:128], in_=psO[:, 0:128], func=AF.Square, accum_out=st[:, 7:8])))())
                    P.step("vector", lambda e: e.tensor_add(out=gst[:, 1:2], in0=gst[:, 1:2], in1=st[:, 7:8]))
                    P.step("vector", (lambda psO=psO, dc=dc, tt=tt: (lambda e: e.tensor_copy(
                        out=oh[:, dc*T + tt*128 : dc*T + (tt+1)*128], in_=psO[:, 0:128])))())
            P.step("tensor", lambda e: e.matmul(psS[0:1, 0:2], onescol[:, :], gst[:, 0:2], start=True, stop=True))
            P.step("vector", lambda e: e.tensor_scalar(out=sc1[0:1, 0:2], in0=psS[0:1, 0:2], scalar1=1.0/(T*256.0), scalar2=None, op0=ALU.mult))
            P.step("vector", lambda e: e.tensor_mul(out=sc1[0:1, 2:3], in0=sc1[0:1, 0:1], in1=sc1[0:1, 0:1]))
            P.step("vector", lambda e: e.tensor_sub(out=sc1[0:1, 3:4], in0=sc1[0:1, 1:2], in1=sc1[0:1, 2:3]))
            P.step("vector", lambda e: e.tensor_scalar(out=sc1[0:1, 3:4], in0=sc1[0:1, 3:4], scalar1=1.0, scalar2=1e-5, op0=ALU.mult, op1=ALU.add))
            P.step("vector", lambda e: e.reciprocal(out=sc1[0:1, 3:4], in_=sc1[0:1, 3:4]))
            P.step("scalar", lambda e: e.activation(out=sc1[0:1, 3:4], in_=sc1[0:1, 3:4], func=AF.Sqrt))
            P.step("tensor", lambda e: e.matmul(psS[:, 4:5], ones1[:, :], sc1[0:1, 0:1], start=True, stop=True))
            P.step("tensor", lambda e: e.matmul(psS[:, 5:6], ones1[:, :], sc1[0:1, 3:4], start=True, stop=True))
            P.step("vector", lambda e: e.tensor_copy(out=lam_s[:, 10:11], in_=psS[:, 4:5]))
            P.step("vector", lambda e: e.tensor_copy(out=lam_s[:, 11:12], in_=psS[:, 5:6]))
            for dc in range(2):
                gw = gnw[:, hh*2+dc : hh*2+dc+1]
                gb = gnb[:, hh*2+dc : hh*2+dc+1]
                P.step("vector", (lambda gw=gw: (lambda e: e.tensor_tensor(out=st[:, 4:5], in0=gw, in1=lam_s[:, 11:12], op=ALU.mult)))())
                P.step("vector", lambda e: e.tensor_scalar(out=st[:, 4:5], in0=st[:, 4:5], scalar1=0.8, scalar2=None, op0=ALU.mult))
                P.step("vector", (lambda gb=gb: (lambda e: e.tensor_tensor(out=st[:, 5:6], in0=lam_s[:, 10:11], in1=st[:, 4:5], op=ALU.mult)))())
                P.step("vector", (lambda gb=gb: (lambda e: e.scalar_tensor_tensor(out=st[:, 5:6], in0=gb, scalar=0.8, in1=st[:, 5:6], op0=ALU.mult, op1=ALU.subtract)))())
                P.step("vector", (lambda dc=dc, hh=hh: (lambda e: e.tensor_scalar(
                    out=oh[:, dc*T:(dc+1)*T], in0=oh[:, dc*T:(dc+1)*T],
                    scalar1=st[:, 4:5], scalar2=st[:, 5:6], op0=ALU.mult, op1=ALU.add)))())
                P.step("gpsimd", (lambda dc=dc, hh=hh: (lambda e: e.dma_start(
                    out=oT_out[hh*256 + dc*128 : hh*256 + (dc+1)*128, :], in_=oh[:, dc*T:(dc+1)*T])))(), 16)
            P.step("vector", lambda e: e.memset(gst[:], 0.0))

        P.emit(block, sem)
    return nc


def run_attn1(x2d, I):
    nc = _get("attn1", build_attn1)
    Wq = np.asarray(I["L1_attn_Wq"], dtype=np.float32)
    Wk = np.asarray(I["L1_attn_Wk"], dtype=np.float32)
    Wv = np.asarray(I["L1_attn_Wv"], dtype=np.float32)
    gnw = np.asarray(I["L1_attn_gn_w"], dtype=np.float32)
    gnb = np.asarray(I["L1_attn_gn_b"], dtype=np.float32)
    lamv = np.concatenate([np.asarray(I[k], dtype=np.float32) for k in
                           ("L1_attn_lambda_q1", "L1_attn_lambda_k1", "L1_attn_lambda_q2", "L1_attn_lambda_k2")])[None, :]
    inv = 1.0 / (10000.0 ** (np.arange(0, 128, 2, dtype=np.float32) / 128.0))
    freqs = np.arange(T, dtype=np.float32)[:, None] * inv[None, :]
    cos = np.cos(freqs).reshape(32, 128, 64).transpose(1, 0, 2).reshape(128, 2048)
    sin = np.sin(freqs).reshape(32, 128, 64).transpose(1, 0, 2).reshape(128, 2048)
    dmask = np.zeros((4, 128, 512), dtype=np.float32)
    for j in range(4):
        tl = np.arange(128)[:, None]; sl = np.arange(512)[None, :]
        dmask[j] = np.where(sl <= tl + j*128, 0.0, -1e9)
    dmask = np.ascontiguousarray(dmask.transpose(1, 0, 2).reshape(128, 4*512))
    identb_np = np.eye(128, dtype=BF16NP)
    in_maps = []
    for c in range(NCORES):
        b, g = c // 2, c % 2
        hs = slice(g*8*256, (g+1)*8*256)
        ks = slice(g*2*256, (g+1)*2*256)
        in_maps.append({
            "x_in": np.ascontiguousarray(x2d[b*T:(b+1)*T]),
            "wq_in": np.ascontiguousarray(Wq[:, hs]),
            "wk_in": np.ascontiguousarray(Wk[:, ks]),
            "wv_in": np.ascontiguousarray(Wv[:, ks]),
            "cos_in": cos, "sin_in": sin, "lamv": lamv,
            "gnw_in": np.ascontiguousarray(gnw[hs].reshape(16, 128).T),
            "gnb_in": np.ascontiguousarray(gnb[hs].reshape(16, 128).T),
            "dmask_in": dmask, "ident_in": identb_np,
        })
    res = run_bass_kernel_spmd(nc, in_maps, core_ids=list(range(NCORES)))
    oT = np.zeros((4096, TOK), dtype=BF16NP)
    for c in range(NCORES):
        b, g = c // 2, c % 2
        oT[g*2048:(g+1)*2048, b*T:(b+1)*T] = res.results[c]["oT_out"]
    return oT


SCALE0 = 1.0 / float(np.sqrt(128.0))
DEBUG_ATTN0 = False

def build_attn0():
    nc = bass.Bass("TRN2", target_bir_lowering=False)
    x_in = nc.dram_tensor("x_in", [T, D], F32, kind="ExternalInput").ap()
    wq_in = nc.dram_tensor("wq_in", [D, 1024], F32, kind="ExternalInput").ap()
    wk_in = nc.dram_tensor("wk_in", [D, 1024], F32, kind="ExternalInput").ap()
    wv_in = nc.dram_tensor("wv_in", [D, 1024], F32, kind="ExternalInput").ap()
    ww_in = nc.dram_tensor("ww_in", [D, 1024], F32, kind="ExternalInput").ap()
    wf1_in = nc.dram_tensor("wf1_in", [D, 128], F32, kind="ExternalInput").ap()
    wf2_in = nc.dram_tensor("wf2_in", [128, 1024], F32, kind="ExternalInput").ap()
    wb_in = nc.dram_tensor("wb_in", [D, 8], F32, kind="ExternalInput").ap()
    wg1_in = nc.dram_tensor("wg1_in", [D, 128], F32, kind="ExternalInput").ap()
    wg2_in = nc.dram_tensor("wg2_in", [128, 1024], F32, kind="ExternalInput").ap()
    bg2_in = nc.dram_tensor("bg2_in", [1, 1024], F32, kind="ExternalInput").ap()
    cq_in = nc.dram_tensor("cq_in", [128, 32], F32, kind="ExternalInput").ap()
    ck_in = nc.dram_tensor("ck_in", [128, 32], F32, kind="ExternalInput").ap()
    cv_in = nc.dram_tensor("cv_in", [128, 32], F32, kind="ExternalInput").ap()
    normw_in = nc.dram_tensor("normw_in", [1, 128], F32, kind="ExternalInput").ap()
    ut1_in = nc.dram_tensor("ut1_in", [64, 64], BF16, kind="ExternalInput").ap()
    mask64_in = nc.dram_tensor("mask64_in", [64, 64], F32, kind="ExternalInput").ap()
    ident_in = nc.dram_tensor("ident_in", [128, 128], BF16, kind="ExternalInput").ap()
    identf_in = nc.dram_tensor("identf_in", [128, 128], F32, kind="ExternalInput").ap()
    o_out = nc.dram_tensor("o_out", [T, 1024], BF16, kind="ExternalOutput").ap()
    _ikind = "ExternalOutput" if DEBUG_ATTN0 else "Internal"
    xhT = nc.dram_tensor("xhT", [D, T], BF16, kind=_ikind).ap()
    qd = nc.dram_tensor("qd", [1024, T], BF16, kind=_ikind).ap()
    kd = nc.dram_tensor("kd", [1024, T], BF16, kind=_ikind).ap()
    vd = nc.dram_tensor("vd", [1024, T], BF16, kind=_ikind).ap()
    wdd = nc.dram_tensor("wdd", [1024, T], BF16, kind=_ikind).ap()
    gd = nc.dram_tensor("gd", [T, 1024], BF16, kind=_ikind).ap()
    sgd = nc.dram_tensor("sgd", [T, 1024], BF16, kind=_ikind).ap()
    betad = nc.dram_tensor("betad", [T, 8], F32, kind=_ikind).ap()

    with ExitStack() as stx:
        cbuf = _sb(nc, stx, "cbuf", [128, 8], F32)
        arena = _sb(nc, stx, "arena", [128, 24576], BF16)
        xt = _sb(nc, stx, "xt", [128, 2048], F32)
        xc = _sb(nc, stx, "xc", [128, 2048], F32)
        sq = _sb(nc, stx, "sq", [128, 2048], F32)
        st = _sb(nc, stx, "st", [128, 8], F32)
        xb = _sb(nc, stx, "xb", [128, 2048], BF16)
        xsT = _sb(nc, stx, "xsT", [128, 2048], BF16)
        identb = _sb(nc, stx, "identb", [128, 128], BF16)
        prebuf = _sb(nc, stx, "prebuf", [128, 520], F32)
        halo = _sb(nc, stx, "halo", [128, 4 * 8 * 4], F32)
        convb = _sb(nc, stx, "convb", [128, 512], F32)
        convo = _sb(nc, stx, "convo", [128, 512], BF16)
        cqb = _sb(nc, stx, "cqb", [128, 32], F32)
        ckb = _sb(nc, stx, "ckb", [128, 32], F32)
        cvb = _sb(nc, stx, "cvb", [128, 32], F32)
        wf1b = _sb(nc, stx, "wf1b", [128, 2048], BF16)
        wf2b = _sb(nc, stx, "wf2b", [128, 1024], BF16)
        wg1b = _sb(nc, stx, "wg1b", [128, 2048], BF16)
        wg2b = _sb(nc, stx, "wg2b", [128, 1024], BF16)
        wbb = _sb(nc, stx, "wbb", [128, 128], BF16)
        bg2bc = _sb(nc, stx, "bg2bc", [128, 1024], F32)
        bg2s = _sb(nc, stx, "bg2s", [1, 1024], F32)
        y1b = _sb(nc, stx, "y1b", [128, 512], BF16)
        gout = _sb(nc, stx, "gout", [128, 4096], BF16)
        sgout = _sb(nc, stx, "sgout", [128, 4096], BF16)
        gtmp = _sb(nc, stx, "gtmp", [128, 512], F32)
        gtmp2 = _sb(nc, stx, "gtmp2", [128, 512], F32)
        ones1 = _sb(nc, stx, "ones1", [1, 128], F32)
        onescol = _sb(nc, stx, "onescol", [128, 1], F32)
        nwrow = _sb(nc, stx, "nwrow", [1, 128], F32)
        ut1 = _sb(nc, stx, "ut1", [64, 64], BF16)
        mask64 = _sb(nc, stx, "mask64", [64, 64], F32)
        nwb = _sb(nc, stx, "nwb", [128, 128], F32)
        identf2 = _sb(nc, stx, "identf2", [128, 128], F32)
        gch2 = _sb(nc, stx, "gch2", [64, 8 * 1024], BF16)
        sgch2 = _sb(nc, stx, "sgch2", [64, 8 * 1024], BF16)
        obuf2 = _sb(nc, stx, "obuf2", [64, 8 * 1024], BF16)
        bsc = _sb(nc, stx, "bsc", [64, 64], F32)
        wtf = _sb(nc, stx, "wtf", [128, 512], F32)
        wfl = _sb(nc, stx, "wfl", [128, 512], F32)
        SkT = _sb(nc, stx, "SkT", [128, 8 * 128], F32)
        Svs = _sb(nc, stx, "Svs", [128, 8 * 128], F32)
        Sk0b = _sb(nc, stx, "Sk0b", [128, 8 * 128], BF16)
        Sv0b = _sb(nc, stx, "Sv0b", [128, 8 * 128], BF16)
        lamb = _sb(nc, stx, "lamb", [128, 64], F32)
        lamT = _sb(nc, stx, "lamT", [64, 128], F32)
        rlam = _sb(nc, stx, "rlam", [128, 64], F32)
        rlam2 = _sb(nc, stx, "rlam2", [128, 64], F32)
        PTb = _sb(nc, stx, "PTb", [128, 64], BF16)
        PTpb = _sb(nc, stx, "PTpb", [128, 64], BF16)
        Pb2 = _sb(nc, stx, "Pb2", [64, 128], BF16)
        Ppb = _sb(nc, stx, "Ppb", [64, 128], BF16)
        Kb2 = _sb(nc, stx, "Kb2", [64, 128], BF16)
        Vb2 = _sb(nc, stx, "Vb2", [64, 128], BF16)
        ATb = _sb(nc, stx, "ATb", [64, 64], BF16)
        AtTb = _sb(nc, stx, "AtTb", [64, 64], BF16)
        aTb = _sb(nc, stx, "aTb", [128, 64], BF16)
        ab2 = _sb(nc, stx, "ab2", [64, 128], BF16)
        Ea = _sb(nc, stx, "Ea", [64, 128], F32)
        ot2 = _sb(nc, stx, "ot2", [64, 128], F32)
        bankF1 = _ps(nc, stx, "bankF1", [128, 512], F32)
        bankF2 = _ps(nc, stx, "bankF2", [128, 512], F32)
        bankB1 = _ps(nc, stx, "bankB1", [128, 512], BF16)
        bankB2 = _ps(nc, stx, "bankB2", [128, 128], BF16)
        sem = stx.enter_context(nc.semaphore("sem"))
        block = stx.enter_context(nc.Block())

        wpb = arena[:, 0:16384]
        xqc = arena[:, 16384:24576]
        qc = arena[:, 0:4096]
        kc = arena[:, 4096:8192]
        vc = arena[:, 8192:12288]
        wc = arena[:, 12288:16384]

        P = Prog()
        for s in register_consts(nc, cbuf, [SCALE0, -1.0, 1.0]):
            P.step(*s)
        P.step("sync", lambda e: e.dma_start(out=identb[:], in_=ident_in[:]), 16)
        P.step("sync", lambda e: e.dma_start(out=identf2[:], in_=identf_in[:]), 16)
        P.step("sync", lambda e: e.dma_start(out=ut1[:], in_=ut1_in[:]), 16)
        P.step("sync", lambda e: e.dma_start(out=mask64[:], in_=mask64_in[:]), 16)
        P.step("sync", lambda e: e.dma_start(out=cqb[:], in_=cq_in[:]), 16)
        P.step("sync", lambda e: e.dma_start(out=ckb[:], in_=ck_in[:]), 16)
        P.step("sync", lambda e: e.dma_start(out=cvb[:], in_=cv_in[:]), 16)
        P.step("sync", lambda e: e.dma_start(out=bg2s[:], in_=bg2_in[:]), 16)
        P.step("sync", lambda e: e.dma_start(out=nwrow[:], in_=normw_in[:]), 16)
        P.step("vector", lambda e: e.memset(ones1[:], 1.0))
        P.step("vector", lambda e: e.memset(onescol[:], 1.0))
        P.step("vector", lambda e: e.memset(halo[:], 0.0))
        P.step("vector", lambda e: e.memset(SkT[:], 0.0))
        P.step("vector", lambda e: e.memset(Svs[:], 0.0))
        P.step("vector", lambda e: e.memset(Sk0b[:], 0.0))
        P.step("vector", lambda e: e.memset(Sv0b[:], 0.0))
        for cc in range(2):
            P.step("tensor", (lambda cc=cc: (lambda e: e.matmul(bankF1[:, 0:512], ones1[:, :], bg2s[0:1, cc*512:(cc+1)*512], start=True, stop=True)))())
            P.step("vector", (lambda cc=cc: (lambda e: e.tensor_copy(out=bg2bc[:, cc*512:(cc+1)*512], in_=bankF1[:, 0:512])))())
        P.step("tensor", lambda e: e.matmul(bankF1[:, 0:128], ones1[:, :], nwrow[0:1, :], start=True, stop=True))
        P.step("vector", lambda e: e.tensor_copy(out=nwb[:], in_=bankF1[:, 0:128]))

        for tt in range(32):
            r0 = tt * 128
            P.step("sync", (lambda r0=r0: (lambda e: e.dma_start(out=xt[:], in_=x_in[r0:r0+128, :])))(), 16)
            P.step("vector", lambda e: e.reduce_sum(out=st[:, 0:1], in_=xt[:], axis=AX.X))
            P.step("vector", lambda e: e.tensor_scalar(out=st[:, 0:1], in0=st[:, 0:1], scalar1=1.0/D, scalar2=None, op0=ALU.mult))
            P.step("vector", lambda e: e.tensor_scalar(out=xc[:], in0=xt[:], scalar1=st[:, 0:1], scalar2=None, op0=ALU.subtract))
            P.step("scalar", lambda e: e.activation(out=sq[:], in_=xc[:], func=AF.Square))
            P.step("vector", lambda e: e.reduce_sum(out=st[:, 1:2], in_=sq[:], axis=AX.X))
            P.step("vector", lambda e: e.tensor_scalar(out=st[:, 1:2], in0=st[:, 1:2], scalar1=1.0/D, scalar2=1e-5, op0=ALU.mult, op1=ALU.add))
            P.step("vector", lambda e: e.reciprocal(out=st[:, 2:3], in_=st[:, 1:2]))
            P.step("scalar", lambda e: e.activation(out=st[:, 3:4], in_=st[:, 2:3], func=AF.Sqrt))
            P.step("vector", lambda e: e.tensor_scalar(out=xb[:], in0=xc[:], scalar1=st[:, 3:4], scalar2=None, op0=ALU.mult))
            for gp in range(4):
                for k in range(4):
                    fc = gp * 4 + k
                    P.step("tensor", (lambda fc=fc, k=k: (lambda e: e.transpose(
                        bankB1[:, k*128:(k+1)*128], xb[:, fc*128:(fc+1)*128], identb[:])))())
                for k in range(4):
                    fc = gp * 4 + k
                    P.step("vector", (lambda fc=fc, k=k: (lambda e: e.tensor_copy(
                        out=xsT[:, fc*128:(fc+1)*128], in_=bankB1[:, k*128:(k+1)*128])))())
            P.step("gpsimd", (lambda tt=tt: (lambda e: e.dma_start(
                out=xhT.rearrange("(fc p) t -> p fc t", p=128)[:, :, tt*128:(tt+1)*128],
                in_=xsT[:].rearrange("p (fc t) -> p fc t", t=128))))(), 16)

        for pi, (w_in, cw, out_d) in enumerate([(wq_in, cqb, qd), (wk_in, ckb, kd), (wv_in, cvb, vd), (ww_in, cvb, wdd)]):
            for fc in range(16):
                P.step("sync", (lambda w_in=w_in, fc=fc: (lambda e: e.dma_start(out=xt[:, 0:1024], in_=w_in[fc*128:(fc+1)*128, :])))(), 16)
                P.step("vector", (lambda fc=fc: (lambda e: e.tensor_copy(out=wpb[:, fc*1024:(fc+1)*1024], in_=xt[:, 0:1024])))())
            for tc in range(8):
                P.step("sync", (lambda tc=tc: (lambda e: e.dma_start(
                    out=xqc.rearrange("p (fc t) -> p fc t", t=512),
                    in_=xhT.rearrange("(fc p) t -> p fc t", p=128)[:, :, tc*512:(tc+1)*512])))(), 16)
                for ct in range(8):
                    P.step("tensor", (lambda ct=ct: (lambda e: mm_group(e, bankF1[:, 0:512],
                        [(wpb[:, fc*1024 + ct*128 : fc*1024 + (ct+1)*128], xqc[:, fc*512:(fc+1)*512]) for fc in range(16)])))())
                    P.step("vector", lambda e: e.tensor_copy(out=prebuf[:, 3:515], in_=bankF1[:, 0:512]))
                    hs0 = (pi*8 + ct) * 4
                    P.step("vector", (lambda hs0=hs0: (lambda e: e.tensor_copy(out=prebuf[:, 0:3], in_=halo[:, hs0:hs0+3])))())
                    P.step("vector", (lambda cw=cw, ct=ct: (lambda e: e.tensor_scalar(
                        out=convb[:], in0=prebuf[:, 0:512], scalar1=cw[:, ct*4+0:ct*4+1], scalar2=None, op0=ALU.mult)))())
                    for i in range(1, 4):
                        P.step("vector", (lambda cw=cw, ct=ct, i=i: (lambda e: e.scalar_tensor_tensor(
                            out=convb[:], in0=prebuf[:, i:512+i], scalar=cw[:, ct*4+i:ct*4+i+1], in1=convb[:],
                            op0=ALU.mult, op1=ALU.add)))())
                    P.step("scalar", lambda e: e.activation(out=convo[:], in_=convb[:], func=AF.Silu))
                    P.step("vector", (lambda hs0=hs0: (lambda e: e.tensor_copy(out=halo[:, hs0:hs0+3], in_=prebuf[:, 512:515])))())
                    P.step("gpsimd", (lambda out_d=out_d, ct=ct, tc=tc: (lambda e: e.dma_start(
                        out=out_d[ct*128:(ct+1)*128, tc*512:(tc+1)*512], in_=convo[:])))(), 16)

        for fc in range(16):
            P.step("sync", (lambda fc=fc: (lambda e: e.dma_start(out=xt[:, 0:128], in_=wf1_in[fc*128:(fc+1)*128, :])))(), 16)
            P.step("vector", (lambda fc=fc: (lambda e: e.tensor_copy(out=wf1b[:, fc*128:(fc+1)*128], in_=xt[:, 0:128])))())
            P.step("sync", (lambda fc=fc: (lambda e: e.dma_start(out=xt[:, 0:128], in_=wg1_in[fc*128:(fc+1)*128, :])))(), 16)
            P.step("vector", (lambda fc=fc: (lambda e: e.tensor_copy(out=wg1b[:, fc*128:(fc+1)*128], in_=xt[:, 0:128])))())
            P.step("sync", (lambda fc=fc: (lambda e: e.dma_start(out=xt[:, 0:8], in_=wb_in[fc*128:(fc+1)*128, :])))(), 16)
            P.step("vector", (lambda fc=fc: (lambda e: e.tensor_copy(out=wbb[:, fc*8:(fc+1)*8], in_=xt[:, 0:8])))())
        P.step("sync", lambda e: e.dma_start(out=xt[:, 0:1024], in_=wf2_in[:, :]), 16)
        P.step("vector", lambda e: e.tensor_copy(out=wf2b[:], in_=xt[:, 0:1024]))
        P.step("sync", lambda e: e.dma_start(out=xt[:, 0:1024], in_=wg2_in[:, :]), 16)
        P.step("vector", lambda e: e.tensor_copy(out=wg2b[:], in_=xt[:, 0:1024]))
        for tc in range(8):
            P.step("sync", (lambda tc=tc: (lambda e: e.dma_start(
                out=xqc.rearrange("p (fc t) -> p fc t", t=512),
                in_=xhT.rearrange("(fc p) t -> p fc t", p=128)[:, :, tc*512:(tc+1)*512])))(), 16)
            P.step("tensor", lambda e: mm_group(e, bankF1[:, 0:512],
                [(wf1b[:, fc*128:(fc+1)*128], xqc[:, fc*512:(fc+1)*512]) for fc in range(16)]))
            P.step("vector", lambda e: e.tensor_copy(out=y1b[:], in_=bankF1[:, 0:512]))
            for tti in range(4):
                for cc2 in range(2):
                    P.step("tensor", (lambda tti=tti, cc2=cc2: (lambda e: e.matmul(
                        bankF2[:, 0:512], y1b[:, tti*128:(tti+1)*128], wf2b[:, cc2*512:(cc2+1)*512], start=True, stop=True)))())
                    P.step("vector", lambda e: e.tensor_scalar_min(out=gtmp[:], in0=bankF2[:, 0:512], scalar1=0.0))
                    P.step("scalar", lambda e: e.activation(out=gtmp2[:], in_=bankF2[:, 0:512], func=AF.Abs))
                    P.step("scalar", lambda e: e.activation(out=gtmp2[:], in_=gtmp2[:], func=AF.Exp, scale=-1.0))
                    P.step("scalar", lambda e: e.activation(out=gtmp2[:], in_=gtmp2[:], func=AF.Ln, bias=1.0))
                    P.step("vector", (lambda tti=tti, cc2=cc2: (lambda e: e.tensor_sub(
                        out=gout[:, tti*1024 + cc2*512 : tti*1024 + (cc2+1)*512], in0=gtmp[:], in1=gtmp2[:])))())
            P.step("gpsimd", (lambda tc=tc: (lambda e: e.dma_start(
                out=gd[tc*512:(tc+1)*512, :].rearrange("(tti p) c -> p tti c", p=128),
                in_=gout.rearrange("p (tti c) -> p tti c", c=1024))))(), 16)
            P.step("tensor", lambda e: mm_group(e, bankF1[:, 0:512],
                [(wg1b[:, fc*128:(fc+1)*128], xqc[:, fc*512:(fc+1)*512]) for fc in range(16)]))
            P.step("vector", lambda e: e.tensor_copy(out=y1b[:], in_=bankF1[:, 0:512]))
            for tti in range(4):
                for cc2 in range(2):
                    P.step("tensor", (lambda tti=tti, cc2=cc2: (lambda e: e.matmul(
                        bankF2[:, 0:512], y1b[:, tti*128:(tti+1)*128], wg2b[:, cc2*512:(cc2+1)*512], start=True, stop=True)))())
                    P.step("vector", (lambda cc2=cc2: (lambda e: e.tensor_add(
                        out=gtmp[:], in0=bankF2[:, 0:512], in1=bg2bc[:, cc2*512:(cc2+1)*512])))())
                    P.step("scalar", (lambda tti=tti, cc2=cc2: (lambda e: e.activation(
                        out=sgout[:, tti*1024 + cc2*512 : tti*1024 + (cc2+1)*512], in_=gtmp[:], func=AF.Sigmoid)))())
            P.step("gpsimd", (lambda tc=tc: (lambda e: e.dma_start(
                out=sgd[tc*512:(tc+1)*512, :].rearrange("(tti p) c -> p tti c", p=128),
                in_=sgout.rearrange("p (tti c) -> p tti c", c=1024))))(), 16)
            for tti in range(4):
                P.step("tensor", (lambda tti=tti: (lambda e: mm_group(e, bankF2[:, 0:8],
                    [(xqc[:, fc*512 + tti*128 : fc*512 + (tti+1)*128], wbb[:, fc*8:(fc+1)*8]) for fc in range(16)])))())
                P.step("scalar", lambda e: e.activation(out=gtmp[:, 0:8], in_=bankF2[:, 0:8], func=AF.Sigmoid))
                P.step("gpsimd", (lambda tc=tc, tti=tti: (lambda e: e.dma_start(
                    out=betad[tc*512 + tti*128 : tc*512 + (tti+1)*128, :], in_=gtmp[:, 0:8])))(), 16)

        for tc in range(8):
            for src_d, dst in ((qd, qc), (kd, kc), (vd, vc), (wdd, wc)):
                P.step("sync", (lambda src_d=src_d, dst=dst, tc=tc: (lambda e: e.dma_start(
                    out=dst.rearrange("p (ct t) -> p ct t", t=512),
                    in_=src_d.rearrange("(ct p) t -> p ct t", p=128)[:, :, tc*512:(tc+1)*512])))(), 16)
            P.step("sync", (lambda tc=tc: (lambda e: e.dma_start(
                out=gch2.rearrange("p (u c) -> p u c", c=1024),
                in_=gd[tc*512:(tc+1)*512, :].rearrange("(u p) c -> p u c", p=64))))(), 16)
            P.step("sync", (lambda tc=tc: (lambda e: e.dma_start(
                out=sgch2.rearrange("p (u c) -> p u c", c=1024),
                in_=sgd[tc*512:(tc+1)*512, :].rearrange("(u p) c -> p u c", p=64))))(), 16)
            P.step("sync", (lambda tc=tc: (lambda e: e.dma_start(
                out=bsc.rearrange("p (u h) -> p u h", h=8),
                in_=betad[tc*512:(tc+1)*512, :].rearrange("(u p) h -> p u h", p=64))))(), 16)
            for hh in range(8):
                hsl = lambda buf: buf[:, hh*512:(hh+1)*512]
                P.step("vector", (lambda hh=hh: (lambda e: e.tensor_copy(out=wfl[:], in_=wc[:, hh*512:(hh+1)*512])))())
                P.step("scalar", lambda e: e.activation(out=convb[:], in_=wfl[:], func=AF.Square))
                P.step("tensor", lambda e: e.matmul(bankF2[0:1, 0:512], onescol[:, :], convb[:], start=True, stop=True))
                P.step("vector", lambda e: e.tensor_scalar(out=gtmp[0:1, 0:512], in0=bankF2[0:1, 0:512], scalar1=1.0, scalar2=1e-6, op0=ALU.mult, op1=ALU.add))
                P.step("vector", lambda e: e.reciprocal(out=gtmp2[0:1, 0:512], in_=gtmp[0:1, 0:512]))
                P.step("scalar", lambda e: e.activation(out=gtmp2[0:1, 0:512], in_=gtmp2[0:1, 0:512], func=AF.Sqrt))
                P.step("tensor", lambda e: e.matmul(bankF2[:, 0:512], ones1[:, :], gtmp2[0:1, 0:512], start=True, stop=True))
                P.step("vector", lambda e: e.tensor_tensor(out=wtf[:], in0=wfl[:], in1=bankF2[:, 0:512], op=ALU.mult))
                for u in range(8):
                    qsl = qc[:, hh*512 + u*64 : hh*512 + (u+1)*64]
                    ksl = kc[:, hh*512 + u*64 : hh*512 + (u+1)*64]
                    vsl = vc[:, hh*512 + u*64 : hh*512 + (u+1)*64]
                    gsl = gch2[0:64, u*1024 + hh*128 : u*1024 + (hh+1)*128]
                    sgsl = sgch2[0:64, u*1024 + hh*128 : u*1024 + (hh+1)*128]
                    bsl = bsc[0:64, u*8 + hh : u*8 + hh + 1]
                    sk_s = SkT[:, hh*128:(hh+1)*128]
                    sv_s = Svs[:, hh*128:(hh+1)*128]
                    sk0_s = Sk0b[:, hh*128:(hh+1)*128]
                    sv0_s = Sv0b[:, hh*128:(hh+1)*128]
                    P.step("tensor", (lambda gsl=gsl: (lambda e: e.matmul(bankF1[:, 0:64], gsl, ut1[:, :], start=True, stop=True)))())
                    P.step("tensor", (lambda gsl=gsl: (lambda e: e.matmul(bankF1[0:64, 64:192], ut1[:, :], gsl, start=True, stop=True)))())
                    P.step("scalar", lambda e: e.activation(out=lamb[:], in_=bankF1[:, 0:64], func=AF.Exp))
                    P.step("scalar", lambda e: e.activation(out=lamT[:], in_=bankF1[0:64, 64:192], func=AF.Exp))
                    P.step("vector", lambda e: e.reciprocal(out=rlam[:], in_=lamb[:]))
                    P.step("vector", lambda e: e.tensor_scalar(out=rlam2[:], in0=rlam[:], scalar1=lamb[:, 63:64], scalar2=None, op0=ALU.mult))
                    P.step("vector", (lambda u=u: (lambda e: e.tensor_tensor(out=PTb[:], in0=wtf[:, u*64:(u+1)*64], in1=rlam[:], op=ALU.mult)))())
                    P.step("vector", (lambda u=u: (lambda e: e.tensor_tensor(out=PTpb[:], in0=wtf[:, u*64:(u+1)*64], in1=rlam2[:], op=ALU.mult)))())
                    P.step("tensor", lambda e: e.transpose(bankB1[0:64, 0:128], PTb[:], identb[:]))
                    P.step("vector", lambda e: e.tensor_copy(out=Pb2[:], in_=bankB1[0:64, 0:128]))
                    P.step("tensor", lambda e: e.transpose(bankB1[0:64, 128:256], PTpb[:], identb[:]))
                    P.step("vector", lambda e: e.tensor_copy(out=Ppb[:], in_=bankB1[0:64, 128:256]))
                    P.step("tensor", (lambda ksl=ksl: (lambda e: e.transpose(bankB1[0:64, 256:384], ksl, identb[:])))())
                    P.step("vector", (lambda bsl=bsl: (lambda e: e.tensor_scalar(out=Kb2[:], in0=bankB1[0:64, 256:384], scalar1=bsl, scalar2=None, op0=ALU.mult)))())
                    P.step("tensor", (lambda vsl=vsl: (lambda e: e.transpose(bankB1[0:64, 384:512], vsl, identb[:])))())
                    P.step("vector", (lambda bsl=bsl: (lambda e: e.tensor_scalar(out=Vb2[:], in0=bankB1[0:64, 384:512], scalar1=bsl, scalar2=None, op0=ALU.mult)))())
                    P.step("tensor", (lambda ksl=ksl, qsl=qsl: (lambda e: e.matmul(bankF1[0:64, 448:512], ksl, qsl, start=True, stop=True)))())
                    P.step("vector", (lambda bsl=bsl: (lambda e: e.scalar_tensor_tensor(out=ATb[:], in0=bankF1[0:64, 448:512], scalar=bsl, in1=mask64[:], op0=ALU.mult, op1=ALU.mult)))())
                    P.step("tensor", (lambda qsl=qsl, sk0_s=sk0_s: (lambda e: (
                        e.matmul(bankF1[0:64, 192:320], ATb[:], Pb2[:], start=True, stop=False),
                        e.matmul(bankF1[0:64, 192:320], qsl, sk0_s, start=False, stop=True))[1]))())
                    P.step("vector", lambda e: e.tensor_tensor(out=Ea[:], in0=bankF1[0:64, 192:320], in1=lamT[:], op=ALU.mult))
                    P.step("vector", lambda e: e.reduce_max(out=st[0:64, 0:1], in_=Ea[:], axis=AX.X))
                    P.step("vector", lambda e: e.tensor_scalar(out=st[0:64, 1:2], in0=st[0:64, 0:1], scalar1=-SCALE0, scalar2=None, op0=ALU.mult))
                    P.step("scalar", lambda e: e.activation(out=Ea[:], in_=Ea[:], func=AF.Exp, scale=SCALE0, bias=st[0:64, 1:2], accum_out=st[0:64, 2:3]))
                    P.step("vector", lambda e: e.reciprocal(out=st[0:64, 3:4], in_=st[0:64, 2:3]))
                    P.step("vector", lambda e: e.tensor_scalar(out=Ea[:], in0=Ea[:], scalar1=st[0:64, 3:4], scalar2=None, op0=ALU.mult))
                    P.step("vector", lambda e: e.tensor_tensor(out=ab2[:], in0=Ea[:], in1=lamT[:], op=ALU.mult))
                    P.step("tensor", lambda e: e.transpose(bankB2[:, 0:64], ab2[:], identb[0:64, 0:64]))
                    P.step("vector", lambda e: e.tensor_copy(out=aTb[:], in_=bankB2[:, 0:64]))
                    P.step("tensor", lambda e: e.matmul(bankF2[0:64, 384:448], PTb[:], aTb[:], start=True, stop=True))
                    P.step("vector", lambda e: e.tensor_tensor(out=AtTb[:], in0=bankF2[0:64, 384:448], in1=mask64[:], op=ALU.mult))
                    P.step("tensor", (lambda sv0_s=sv0_s: (lambda e: (
                        e.matmul(bankF1[0:64, 320:448], AtTb[:], Vb2[:], start=True, stop=False),
                        e.matmul(bankF1[0:64, 320:448], aTb[:], sv0_s, start=False, stop=True))[1]))())
                    P.step("vector", lambda e: e.tensor_copy(out=ot2[:], in_=bankF1[0:64, 320:448]))
                    P.step("scalar", lambda e: e.activation(out=Ea[:], in_=ot2[:], func=AF.Square))
                    P.step("vector", lambda e: e.reduce_sum(out=st[0:64, 4:5], in_=Ea[:], axis=AX.X))
                    P.step("vector", lambda e: e.tensor_scalar(out=st[0:64, 4:5], in0=st[0:64, 4:5], scalar1=1.0/128.0, scalar2=1e-5, op0=ALU.mult, op1=ALU.add))
                    P.step("vector", lambda e: e.reciprocal(out=st[0:64, 5:6], in_=st[0:64, 4:5]))
                    P.step("scalar", lambda e: e.activation(out=st[0:64, 5:6], in_=st[0:64, 5:6], func=AF.Sqrt))
                    P.step("vector", lambda e: e.tensor_scalar(out=ot2[:], in0=ot2[:], scalar1=st[0:64, 5:6], scalar2=None, op0=ALU.mult))
                    P.step("vector", lambda e: e.tensor_tensor(out=ot2[:], in0=ot2[:], in1=nwb[0:64, :], op=ALU.mult))
                    P.step("vector", (lambda sgsl=sgsl: (lambda e: e.tensor_copy(out=gtmp2[0:64, 0:128], in_=sgsl)))())
                    P.step("vector", (lambda u=u, hh=hh: (lambda e: e.tensor_tensor(
                        out=obuf2[0:64, u*1024 + hh*128 : u*1024 + (hh+1)*128], in0=ot2[:], in1=gtmp2[0:64, 0:128], op=ALU.mult)))())
                    P.step("tensor", lambda e: e.matmul(bankF2[:, 0:128], Ppb[:], Kb2[:], start=True, stop=True))
                    P.step("vector", (lambda sk_s=sk_s: (lambda e: e.scalar_tensor_tensor(
                        out=sk_s, in0=sk_s, scalar=lamb[:, 63:64], in1=bankF2[:, 0:128], op0=ALU.mult, op1=ALU.add)))())
                    P.step("tensor", lambda e: e.matmul(bankF2[:, 128:256], Ppb[:], Vb2[:], start=True, stop=True))
                    P.step("vector", (lambda sv_s=sv_s: (lambda e: e.scalar_tensor_tensor(
                        out=sv_s, in0=sv_s, scalar=lamb[:, 63:64], in1=bankF2[:, 128:256], op0=ALU.mult, op1=ALU.add)))())
                    P.step("tensor", (lambda sk_s=sk_s: (lambda e: e.transpose(bankF2[:, 256:384], sk_s, identf2[:])))())
                    P.step("vector", (lambda sk0_s=sk0_s: (lambda e: e.tensor_copy(out=sk0_s, in_=bankF2[:, 256:384])))())
                    P.step("vector", (lambda sv_s=sv_s, sv0_s=sv0_s: (lambda e: e.tensor_copy(out=sv0_s, in_=sv_s)))())
            P.step("gpsimd", (lambda tc=tc: (lambda e: e.dma_start(
                out=o_out[tc*512:(tc+1)*512, :].rearrange("(u p) c -> p u c", p=64),
                in_=obuf2.rearrange("p (u c) -> p u c", c=1024))))(), 16)

        P.emit(block, sem)
    return nc


def run_attn0(x2d, I):
    nc = _get("attn0", build_attn0)
    identb_np = np.eye(128, dtype=BF16NP)
    identf_np = np.eye(128, dtype=np.float32)
    ut1_np = np.triu(np.ones((64, 64))).astype(BF16NP)
    mask64_np = np.triu(np.ones((64, 64))).astype(np.float32)
    Wq = np.asarray(I["L0_attn_Wq"], dtype=np.float32)
    Wk = np.asarray(I["L0_attn_Wk"], dtype=np.float32)
    Wv = np.asarray(I["L0_attn_Wv"], dtype=np.float32)
    Ww = np.asarray(I["L0_attn_Ww"], dtype=np.float32)
    Wf2 = np.asarray(I["L0_attn_Wf2"], dtype=np.float32)
    Wb = np.asarray(I["L0_attn_Wb"], dtype=np.float32)
    Wg2 = np.asarray(I["L0_attn_Wg2"], dtype=np.float32)
    bg2 = np.asarray(I["L0_attn_bg2"], dtype=np.float32)
    cq = np.asarray(I["L0_attn_cq"], dtype=np.float32)
    ck = np.asarray(I["L0_attn_ck"], dtype=np.float32)
    cv = np.asarray(I["L0_attn_cv"], dtype=np.float32)
    def convlay(cm, sl):
        c = cm[sl]
        return np.ascontiguousarray(c.reshape(8, 128, 4).transpose(1, 0, 2).reshape(128, 32))
    in_maps = []
    for c in range(NCORES):
        b, g = c // 2, c % 2
        cs = slice(g*1024, (g+1)*1024)
        hs8 = slice(g*8, (g+1)*8)
        in_maps.append({
            "x_in": np.ascontiguousarray(x2d[b*T:(b+1)*T]),
            "wq_in": np.ascontiguousarray(Wq[:, cs]),
            "wk_in": np.ascontiguousarray(Wk[:, cs]),
            "wv_in": np.ascontiguousarray(Wv[:, cs]),
            "ww_in": np.ascontiguousarray(Ww[:, cs]),
            "wf1_in": np.ascontiguousarray(np.asarray(I["L0_attn_Wf1"], dtype=np.float32)),
            "wf2_in": np.ascontiguousarray(Wf2[:, cs]),
            "wb_in": np.ascontiguousarray(Wb[:, hs8]),
            "wg1_in": np.ascontiguousarray(np.asarray(I["L0_attn_Wg1"], dtype=np.float32)),
            "wg2_in": np.ascontiguousarray(Wg2[:, cs]),
            "bg2_in": np.ascontiguousarray(bg2[None, cs]),
            "cq_in": convlay(cq, cs), "ck_in": convlay(ck, cs), "cv_in": convlay(cv, cs),
            "normw_in": np.asarray(I["L0_attn_norm_w"], dtype=np.float32)[None, :],
            "ut1_in": ut1_np, "mask64_in": mask64_np, "ident_in": identb_np,
            "identf_in": identf_np,
        })
    res = run_bass_kernel_spmd(nc, in_maps, core_ids=list(range(NCORES)))
    o = np.zeros((TOK, 2048), dtype=BF16NP)
    for c in range(NCORES):
        b, g = c // 2, c % 2
        o[b*T:(b+1)*T, g*1024:(g+1)*1024] = res.results[c]["o_out"]
    if DEBUG_ATTN0:
        return o, res
    return o


_BUILD_CACHE = {}

def _get(name, builder):
    if name not in _BUILD_CACHE:
        _BUILD_CACHE[name] = builder()
    return _BUILD_CACHE[name]


def run_ffn(kind, x2d, dq, oT=None, wo=None):
    L0 = (kind == "L0")
    od = 0 if oT is None else oT.shape[0]
    nc = _get(f"ffn_{kind}_{od}", lambda: build_ffn(kind, od))
    identb_np = np.eye(128, dtype=BF16NP)
    identf_np = np.eye(128, dtype=np.float32)
    in_maps = []
    for c in range(NCORES):
        m = {
            "x_in": np.ascontiguousarray(x2d[c*TC:(c+1)*TC]),
            "ident_in": identb_np,
            "identf_in": identf_np,
            "wgb": dq["wg" if L0 else "g1"],
            "wub": dq["wu" if L0 else "u1"],
            "wdb": dq["wd" if L0 else "dn1"],
        }
        if od:
            m["oT_in"] = np.ascontiguousarray(oT[:, c*TC:(c+1)*TC])
            m["wo_in"] = wo
        in_maps.append(m)
    res = run_bass_kernel_spmd(nc, in_maps, core_ids=list(range(NCORES)))
    return np.concatenate([res.results[c]["x_out"] for c in range(NCORES)], axis=0)


def kernel(**inputs):
    I = {k: np.asarray(v) for k, v in inputs.items()}
    x = I["L0_attn_hidden_states"].astype(np.float32)
    x2d = np.ascontiguousarray(x.reshape(TOK, D))

    nc0 = _get("dequant", build_dequant)
    res0 = run_bass_kernel_spmd(nc0, [dequant_inputs(c, I) for c in range(NCORES)],
                                core_ids=list(range(NCORES)))
    dq = assemble_dequant([res0.results[c] for c in range(NCORES)])

    x2d = run_ffn("L0", x2d, dq)
    o0 = run_attn0(x2d, I)
    oT0 = np.ascontiguousarray(o0.T)
    x2d = run_ffn("L0", x2d, dq, oT=oT0, wo=dq["wo0"])
    x2d = run_ffn("L1", x2d, dq)
    oT1 = run_attn1(x2d, I)
    x2d = run_ffn("L1", x2d, dq, oT=oT1, wo=dq["wo1"])
    return np.ascontiguousarray(x2d.reshape(B, T, D)).astype(np.float32)


# version 02
# speedup vs baseline: 1.1168x; 1.1168x over previous
import numpy as np
import ml_dtypes
import concourse.bass as bass
import concourse.mybir as mybir
from concourse.bass_utils import run_bass_kernel_spmd
from concourse.alu_op_type import AluOpType as ALU

F32 = mybir.dt.float32
BF16 = mybir.dt.bfloat16
I32 = mybir.dt.int32
AF = mybir.ActivationFunctionType
AX = mybir.AxisListType
BF16NP = ml_dtypes.bfloat16

NCORES = 8

from contextlib import ExitStack

def _sb(nc, st, name, shape, dtype):
    return st.enter_context(nc.sbuf_tensor(name, shape, dtype))

def _ps(nc, st, name, shape, dtype):
    return st.enter_context(nc.psum_tensor(name, shape, dtype))

B, T, D = 4, 4096, 2048
TOK = B * T
TC = TOK // NCORES

ENGINES = ["sync", "vector", "scalar", "tensor", "gpsimd"]

class Prog:
    def __init__(self):
        self.steps = []
        self.wcount = 0

    def step(self, eng, fn, inc=1, dep=None, wwait=None):
        assert eng in ENGINES
        self.steps.append(dict(eng=eng, fn=fn, inc=inc, main=True, dep=dep, wwait=wwait))
        return len(self.steps) - 1

    def wstep(self, eng, fn, dep=None):
        assert eng in ENGINES
        self.steps.append(dict(eng=eng, fn=fn, inc=16, main=False, dep=dep, wwait=None))
        self.wcount += 16
        return self.wcount

    def emit(self, block, sem, wsem=None):
        cum = []
        c = 0
        for st in self.steps:
            if st["main"]:
                c += st["inc"]
            cum.append(c)
        total = c
        wtotal = self.wcount
        if wtotal:
            assert wsem is not None

        def mk(engine_name):
            def body(eng):
                last_main = -1
                last_w = -1
                for i, st in enumerate(self.steps):
                    if st["eng"] != engine_name:
                        continue
                    if st["dep"] is not None:
                        want = cum[st["dep"]]
                    else:
                        want = cum[i] - (st["inc"] if st["main"] else 0)
                    if want > last_main:
                        eng.wait_ge(sem, want)
                        last_main = want
                    if st["wwait"] is not None and st["wwait"] > last_w:
                        eng.wait_ge(wsem, st["wwait"])
                        last_w = st["wwait"]
                    instr = st["fn"](eng)
                    instr.then_inc(sem if st["main"] else wsem, st["inc"])
                if engine_name == "sync":
                    eng.wait_ge(sem, total)
                    if wtotal:
                        eng.wait_ge(wsem, wtotal)
            return body
        for e in ENGINES:
            getattr(block, e)(mk(e))
        return total


def register_consts(nc, cbuf, values):
    steps = []
    for i, v in enumerate(values):
        nc.const_aps.register(F32, float(v), cbuf[:, i:i+1])
        steps.append(("vector", (lambda i=i, v=v: (lambda e: e.memset(cbuf[:, i:i+1], float(v))))(), 1))
    return steps


def mm_group(eng, psum_ap, pairs):
    n = len(pairs)
    instr = None
    for i, (l, r) in enumerate(pairs):
        instr = eng.matmul(psum_ap, l, r, start=(i == 0), stop=(i == n - 1))
    return instr


def build_dequant():
    nc = bass.Bass("TRN2", target_bir_lowering=False)
    wg_pk = nc.dram_tensor("wg_pk", [1024, 2048], I32, kind="ExternalInput").ap()
    wu_pk = nc.dram_tensor("wu_pk", [1024, 2048], I32, kind="ExternalInput").ap()
    wd_pk = nc.dram_tensor("wd_pk", [8192, 256], I32, kind="ExternalInput").ap()
    wg_sc = nc.dram_tensor("wg_sc", [8, 8 * 2048], F32, kind="ExternalInput").ap()
    wu_sc = nc.dram_tensor("wu_sc", [8, 8 * 2048], F32, kind="ExternalInput").ap()
    wd_sc = nc.dram_tensor("wd_sc", [8, 64 * 256], F32, kind="ExternalInput").ap()
    e8 = nc.dram_tensor("e8", [8, 128], F32, kind="ExternalInput").ap()
    g1_w = nc.dram_tensor("g1_w", [2048, 1024], I32, kind="ExternalInput").ap()
    u1_w = nc.dram_tensor("u1_w", [2048, 1024], I32, kind="ExternalInput").ap()
    d1_w = nc.dram_tensor("d1_w", [8192, 256], I32, kind="ExternalInput").ap()
    g1_sc = nc.dram_tensor("g1_sc", [1, 1024], F32, kind="ExternalInput").ap()
    u1_sc = nc.dram_tensor("u1_sc", [1, 1024], F32, kind="ExternalInput").ap()
    d1_sc = nc.dram_tensor("d1_sc", [1, 256], F32, kind="ExternalInput").ap()
    wo0_f = nc.dram_tensor("wo0_f", [2048, 256], F32, kind="ExternalInput").ap()
    wo1_f = nc.dram_tensor("wo1_f", [4096, 256], F32, kind="ExternalInput").ap()
    d_wg = nc.dram_tensor("d_wg", [16, 128, 16, 128], BF16, kind="ExternalOutput").ap()
    d_wu = nc.dram_tensor("d_wu", [16, 128, 16, 128], BF16, kind="ExternalOutput").ap()
    d_wd = nc.dram_tensor("d_wd", [2, 128, 128, 128], BF16, kind="ExternalOutput").ap()
    d_g1 = nc.dram_tensor("d_g1", [8, 128, 16, 128], BF16, kind="ExternalOutput").ap()
    d_u1 = nc.dram_tensor("d_u1", [8, 128, 16, 128], BF16, kind="ExternalOutput").ap()
    d_d1 = nc.dram_tensor("d_d1", [2, 128, 64, 128], BF16, kind="ExternalOutput").ap()
    d_wo0 = nc.dram_tensor("d_wo0", [2048, 256], BF16, kind="ExternalOutput").ap()
    d_wo1 = nc.dram_tensor("d_wo1", [4096, 256], BF16, kind="ExternalOutput").ap()

    with ExitStack() as stx:
        cbuf = _sb(nc, stx, "cbuf", [128, 8], F32)
        pk = _sb(nc, stx, "pk", [128, 2048], I32)
        klo = _sb(nc, stx, "klo", [128, 512], I32)
        khi = _sb(nc, stx, "khi", [128, 512], I32)
        hi_t = _sb(nc, stx, "hi", [128, 512], I32)
        blo = _sb(nc, stx, "blo", [128, 512], I32)
        bhi = _sb(nc, stx, "bhi", [128, 512], I32)
        kf = _sb(nc, stx, "kf", [128, 512], F32)
        r1 = _sb(nc, stx, "r1", [128, 512], F32)
        r2 = _sb(nc, stx, "r2", [128, 512], F32)
        mag = _sb(nc, stx, "mag", [128, 512], F32)
        sgn = _sb(nc, stx, "sgn", [128, 512], F32)
        vlo = _sb(nc, stx, "vlo", [128, 512], BF16)
        vhi = _sb(nc, stx, "vhi", [128, 512], BF16)
        e8sb = _sb(nc, stx, "e8sb", [8, 128], F32)
        scsb = _sb(nc, stx, "scsb", [8, 16384], F32)
        ones1 = _sb(nc, stx, "ones1", [1, 128], F32)
        sc1 = _sb(nc, stx, "sc1", [1, 1024], F32)
        it_t = _sb(nc, stx, "it", [128, 512], I32)
        ft = _sb(nc, stx, "ft", [128, 512], F32)
        bt = _sb(nc, stx, "bt", [128, 512], BF16)
        scps = _ps(nc, stx, "scps", [128, 512], F32)
        bcps = _ps(nc, stx, "bcps", [128, 512], F32)
        sem = stx.enter_context(nc.semaphore("sem"))
        block = stx.enter_context(nc.Block())

        P = Prog()
        for st in register_consts(nc, cbuf, [0.5, -2.0, -6.0, 1.0, -0.25, -128.0]):
            P.step(*st)
        P.step("sync", lambda e: e.dma_start(out=e8sb[:], in_=e8[:]), 16)
        P.step("vector", lambda e: e.memset(ones1[:], 1.0))

        def nv4_matrix(pk_d, sc_d, out_d, KP, NC_, KC, a_scale_rows):
            ntile_a = KP // 128
            P.step("sync", lambda e: e.dma_start(out=scsb[:8, :a_scale_rows * NC_], in_=sc_d[:, :a_scale_rows * NC_]), 16)
            for a in range(ntile_a):
                P.step("sync", (lambda a=a: (lambda e: e.dma_start(out=pk[:, :NC_], in_=pk_d[a*128:(a+1)*128, :])))(), 16)
                for c0 in range(0, NC_, 512):
                    w = min(512, NC_ - c0)
                    pkv = pk[:, c0:c0+w]
                    P.step("tensor", (lambda a=a, c0=c0, w=w: (lambda e: e.matmul(
                        scps[:, :w], e8sb[:, :], scsb[0:8, a*NC_+c0 : a*NC_+c0+w], start=True, stop=True)))())
                    P.step("vector", (lambda pkv=pkv, w=w: (lambda e: e.tensor_scalar(out=hi_t[:, :w], in0=pkv, scalar1=4, scalar2=None, op0=ALU.logical_shift_right)))())
                    P.step("vector", (lambda pkv=pkv, w=w: (lambda e: e.tensor_scalar(out=klo[:, :w], in0=pkv, scalar1=7, scalar2=None, op0=ALU.bitwise_and)))())
                    P.step("vector", (lambda pkv=pkv, w=w: (lambda e: e.tensor_scalar(out=blo[:, :w], in0=pkv, scalar1=8, scalar2=None, op0=ALU.bitwise_and)))())
                    P.step("vector", (lambda w=w: (lambda e: e.tensor_scalar(out=khi[:, :w], in0=hi_t[:, :w], scalar1=7, scalar2=None, op0=ALU.bitwise_and)))())
                    P.step("vector", (lambda w=w: (lambda e: e.tensor_scalar(out=bhi[:, :w], in0=hi_t[:, :w], scalar1=8, scalar2=None, op0=ALU.bitwise_and)))())
                    for (kk, bb, vv, nib) in ((klo, blo, vlo, 0), (khi, bhi, vhi, 1)):
                        P.step("vector", (lambda kk=kk, w=w: (lambda e: e.tensor_copy(out=kf[:, :w], in_=kk[:, :w])))())
                        P.step("scalar", (lambda w=w: (lambda e: e.activation(out=r1[:, :w], in_=kf[:, :w], func=AF.Relu, scale=0.5, bias=-2.0)))())
                        P.step("scalar", (lambda w=w: (lambda e: e.activation(out=r2[:, :w], in_=kf[:, :w], func=AF.Relu, bias=-6.0)))())
                        P.step("vector", (lambda w=w: (lambda e: e.scalar_tensor_tensor(out=mag[:, :w], in0=kf[:, :w], scalar=0.5, in1=r1[:, :w], op0=ALU.mult, op1=ALU.add)))())
                        P.step("vector", (lambda w=w: (lambda e: e.tensor_add(out=mag[:, :w], in0=mag[:, :w], in1=r2[:, :w])))())
                        P.step("scalar", (lambda bb=bb, w=w: (lambda e: e.activation(out=sgn[:, :w], in_=bb[:, :w], func=AF.Copy, scale=-0.25, bias=1.0)))())
                        P.step("vector", (lambda w=w: (lambda e: e.tensor_mul(out=mag[:, :w], in0=mag[:, :w], in1=sgn[:, :w])))())
                        P.step("vector", (lambda vv=vv, w=w: (lambda e: e.tensor_tensor(out=vv[:, :w], in0=mag[:, :w], in1=scps[:, :w], op=ALU.mult)))())
                        nslab = w // 128
                        s0 = c0 // 128
                        src_lo = vv[0:64, :w].rearrange("p (s n) -> p s n", n=128)
                        src_hi = vv[64:128, :w].rearrange("p (s n) -> p s n", n=128)
                        dst_lo = out_d[s0:s0+nslab, nib:128:2, 2*a, :].rearrange("s p n -> p s n")
                        dst_hi = out_d[s0:s0+nslab, nib:128:2, 2*a+1, :].rearrange("s p n -> p s n")
                        P.step("gpsimd", (lambda src=src_lo, dst=dst_lo: (lambda e: e.dma_start(out=dst, in_=src)))(), 16)
                        P.step("gpsimd", (lambda src=src_hi, dst=dst_hi: (lambda e: e.dma_start(out=dst, in_=src)))(), 16)

        nv4_matrix(wg_pk, wg_sc, d_wg, 1024, 2048, 16, 8)
        nv4_matrix(wu_pk, wu_sc, d_wu, 1024, 2048, 16, 8)
        nv4_matrix(wd_pk, wd_sc, d_wd, 8192, 256, 128, 64)

        def int8_matrix(w_d, sc_d, out_d, K, NC_, scn):
            P.step("sync", (lambda: (lambda e: e.dma_start(out=sc1[:, :scn], in_=sc_d[:, :scn])))(), 16)
            for kc in range(K // 128):
                P.step("sync", (lambda kc=kc: (lambda e: e.dma_start(out=pk[:, :NC_], in_=w_d[kc*128:(kc+1)*128, :])))(), 16)
                for c0 in range(0, NC_, 512):
                    w = min(512, NC_ - c0)
                    P.step("tensor", (lambda c0=c0, w=w: (lambda e: e.matmul(
                        bcps[:, :w], ones1[:, :], sc1[:, c0:c0+w], start=True, stop=True)))())
                    P.step("vector", (lambda c0=c0, w=w: (lambda e: e.tensor_copy(out=ft[:, :w], in_=pk[:, c0:c0+w])))())
                    P.step("vector", (lambda w=w: (lambda e: e.scalar_tensor_tensor(out=bt[:, :w], in0=ft[:, :w], scalar=-128.0, in1=bcps[:, :w], op0=ALU.add, op1=ALU.mult)))())
                    nslab = w // 128
                    s0 = c0 // 128
                    src = bt[:, :w].rearrange("p (s n) -> p s n", n=128)
                    dst = out_d[s0:s0+nslab, :, kc, :].rearrange("s p n -> p s n")
                    P.step("gpsimd", (lambda src=src, dst=dst: (lambda e: e.dma_start(out=dst, in_=src)))(), 16)

        int8_matrix(g1_w, g1_sc, d_g1, 2048, 1024, 1024)
        int8_matrix(u1_w, u1_sc, d_u1, 2048, 1024, 1024)
        int8_matrix(d1_w, d1_sc, d_d1, 8192, 256, 256)

        def cast_matrix(w_d, out_d, K, NC_):
            for kc in range(K // 128):
                P.step("sync", (lambda kc=kc: (lambda e: e.dma_start(out=ft[:, :NC_], in_=w_d[kc*128:(kc+1)*128, :])))(), 16)
                P.step("vector", (lambda: (lambda e: e.tensor_copy(out=bt[:, :NC_], in_=ft[:, :NC_])))())
                P.step("gpsimd", (lambda kc=kc: (lambda e: e.dma_start(out=out_d[kc*128:(kc+1)*128, :], in_=bt[:, :NC_])))(), 16)

        cast_matrix(wo0_f, d_wo0, 2048, 256)
        cast_matrix(wo1_f, d_wo1, 4096, 256)

        P.emit(block, sem)
    return nc


def dequant_inputs(core, I):
    tp = lambda x: np.ascontiguousarray(np.asarray(x).T)
    c = core
    wgT = tp(I["L0_ffn_gate_W_packed"])
    wuT = tp(I["L0_ffn_up_W_packed"])
    wdT = tp(I["L0_ffn_down_W_packed"])
    def screg(sc, a_rows):
        scT = tp(sc)
        KB, N = scT.shape
        assert KB == 8 * a_rows
        out = scT.reshape(a_rows, 8, N).transpose(1, 0, 2).reshape(8, a_rows * N)
        return np.ascontiguousarray(out)
    gsc = screg(I["L0_ffn_gate_scales"], 8)
    usc = screg(I["L0_ffn_up_scales"], 8)
    dsc = screg(I["L0_ffn_down_scales"], 64)
    e8 = np.zeros((8, 128), dtype=np.float32)
    for r in range(8):
        e8[r, r*16:(r+1)*16] = 1.0
    g1T = tp(I["L1_ffn_gate_w"])
    u1T = tp(I["L1_ffn_up_w"])
    d1T = tp(I["L1_ffn_down_w"])
    ns = slice(c*2048, (c+1)*2048)
    hs = slice(c*256, (c+1)*256)
    i1s = slice(c*1024, (c+1)*1024)
    return {
        "wg_pk": np.ascontiguousarray(wgT[:, ns]),
        "wu_pk": np.ascontiguousarray(wuT[:, ns]),
        "wd_pk": np.ascontiguousarray(wdT[:, hs]),
        "wg_sc": np.ascontiguousarray(gsc.reshape(8, 8, 16384)[:, :, ns].reshape(8, 8*2048)),
        "wu_sc": np.ascontiguousarray(usc.reshape(8, 8, 16384)[:, :, ns].reshape(8, 8*2048)),
        "wd_sc": np.ascontiguousarray(dsc.reshape(8, 64, 2048)[:, :, hs].reshape(8, 64*256)),
        "e8": e8,
        "g1_w": np.ascontiguousarray(g1T[:, i1s]).astype(np.int32),
        "u1_w": np.ascontiguousarray(u1T[:, i1s]).astype(np.int32),
        "d1_w": np.ascontiguousarray(d1T[:, hs]).astype(np.int32),
        "g1_sc": np.asarray(I["L1_ffn_gate_scale"])[None, i1s].astype(np.float32),
        "u1_sc": np.asarray(I["L1_ffn_up_scale"])[None, i1s].astype(np.float32),
        "d1_sc": np.asarray(I["L1_ffn_down_scale"])[None, hs].astype(np.float32),
        "wo0_f": np.ascontiguousarray(np.asarray(I["L0_attn_Wo"])[:, hs]).astype(np.float32),
        "wo1_f": np.ascontiguousarray(np.asarray(I["L1_attn_Wo"])[:, hs]).astype(np.float32),
    }


def assemble_dequant(res):
    cat = lambda key, axis: np.concatenate([res[c][key] for c in range(NCORES)], axis=axis)
    return {
        "wg": cat("d_wg", 0),
        "wu": cat("d_wu", 0),
        "wd": cat("d_wd", 0),
        "g1": cat("d_g1", 0),
        "u1": cat("d_u1", 0),
        "dn1": cat("d_d1", 0),
        "wo0": cat("d_wo0", 1),
        "wo1": cat("d_wo1", 1),
    }


def build_ffn(kind, prologue_od):
    L0 = (kind == "L0")
    I = 16384 if L0 else 8192
    NSTRIP = I // 2048
    nc = bass.Bass("TRN2", target_bir_lowering=False)
    x_in = nc.dram_tensor("x_in", [TC, D], F32, kind="ExternalInput").ap()
    ident_in = nc.dram_tensor("ident_in", [128, 128], BF16, kind="ExternalInput").ap()
    identf_in = nc.dram_tensor("identf_in", [128, 128], F32, kind="ExternalInput").ap()
    wgb = nc.dram_tensor("wgb", [I // 128, 128, 16, 128], BF16, kind="ExternalInput").ap()
    wub = nc.dram_tensor("wub", [I // 128, 128, 16, 128], BF16, kind="ExternalInput").ap()
    wdb = nc.dram_tensor("wdb", [16, 128, I // 128, 128], BF16, kind="ExternalInput").ap()
    x_out = nc.dram_tensor("x_out", [TC, D], F32, kind="ExternalOutput").ap()
    if prologue_od:
        OD = prologue_od
        oT_in = nc.dram_tensor("oT_in", [OD, TC], BF16, kind="ExternalInput").ap()
        wo_in = nc.dram_tensor("wo_in", [OD, D], BF16, kind="ExternalInput").ap()
        xprime = nc.dram_tensor("xprime", [TC, D], F32, kind="Internal").ap()
        xbase = xprime
    else:
        OD = 0
        xbase = x_in

    with ExitStack() as stx:
        cbuf = _sb(nc, stx, "cbuf", [128, 8], F32)
        yT = _sb(nc, stx, "yT", [128, 16 * 1024], BF16)
        inter = _sb(nc, stx, "inter", [128, 16 * 1024], BF16)
        acc = _sb(nc, stx, "acc", [128, 16 * 1024], F32)
        wsl = [_sb(nc, stx, "wslA", [128, 2048], BF16),
               _sb(nc, stx, "wslB", [128, 2048], BF16)]
        wsl2 = [_sb(nc, stx, "wsl2A", [128, 2048], BF16),
                _sb(nc, stx, "wsl2B", [128, 2048], BF16)]
        wdl = [_sb(nc, stx, "wdlA", [128, 2048], BF16),
               _sb(nc, stx, "wdlB", [128, 2048], BF16)]
        xt = _sb(nc, stx, "xt", [128, 2048], F32)
        xc = _sb(nc, stx, "xc", [128, 2048], F32)
        xb = _sb(nc, stx, "xb", [128, 2048], BF16)
        st = _sb(nc, stx, "st", [128, 4], F32)
        identb = _sb(nc, stx, "ident", [128, 128], BF16)
        identf = _sb(nc, stx, "identf", [128, 128], F32)
        g1t = _sb(nc, stx, "g1t", [128, 512], F32)
        g2t = _sb(nc, stx, "g2t", [128, 512], F32)
        g3t = _sb(nc, stx, "g3t", [128, 512], F32)
        outt = _sb(nc, stx, "outt", [128, 2048], F32)
        sq = outt
        wot = _sb(nc, stx, "wot", [128, 512], BF16)
        otb = _sb(nc, stx, "otb", [128, 512], BF16)
        psg = _ps(nc, stx, "psg", [128, 512], F32)
        psu = _ps(nc, stx, "psu", [128, 512], F32)
        psd = _ps(nc, stx, "psd", [128, 512], F32)
        pstb = _ps(nc, stx, "pst", [128, 512], BF16)
        pstf = _ps(nc, stx, "pstf", [128, 512], F32)
        pswo = _ps(nc, stx, "pswo", [128, 512], F32)
        sem = stx.enter_context(nc.semaphore("sem"))
        wsem = stx.enter_context(nc.semaphore("wsem"))
        block = stx.enter_context(nc.Block())

        P = Prog()
        for s in register_consts(nc, cbuf, [0.044715, 0.7978845608028654, 1e-5, 1.0/D, 0.5]):
            P.step(*s)
        P.step("sync", lambda e: e.dma_start(out=identb[:], in_=ident_in[:]), 16)
        P.step("sync", lambda e: e.dma_start(out=identf[:], in_=identf_in[:]), 16)

        NODC = OD // 128 if OD else 0

        def prologue():
            psbanks = [psg, psu, psd, pstf, pswo, pstb]
            banks = [psg, psu, psd, pswo]
            for fc in range(4):
                for ttg in range(4):
                    tts = [ttg * 4 + k for k in range(4)]
                    for odc in range(NODC):
                        P.step("sync", (lambda odc=odc, fc=fc: (lambda e: e.dma_start(
                            out=wot[:, 0:512], in_=wo_in[odc*128:(odc+1)*128, fc*512:(fc+1)*512])))(), 16)
                        for k, tt in enumerate(tts):
                            P.step("sync", (lambda odc=odc, tt=tt, k=k: (lambda e: e.dma_start(
                                out=otb[:, k*128:(k+1)*128], in_=oT_in[odc*128:(odc+1)*128, tt*128:(tt+1)*128])))(), 16)
                        for k in range(4):
                            P.step("tensor", (lambda k=k, odc=odc: (lambda e: e.matmul(
                                banks[k][:, :], otb[:, k*128:(k+1)*128], wot[:, 0:512],
                                start=(odc == 0), stop=(odc == NODC - 1))))())
                    for k, tt in enumerate(tts):
                        P.step("sync", (lambda tt=tt, fc=fc: (lambda e: e.dma_start(
                            out=xt[:, 0:512], in_=x_in[tt*128:(tt+1)*128, fc*512:(fc+1)*512])))(), 16)
                        P.step("vector", (lambda k=k: (lambda e: e.scalar_tensor_tensor(
                            out=outt[:, 0:512], in0=banks[k][:, :], scalar=1.0, in1=xt[:, 0:512],
                            op0=ALU.mult, op1=ALU.add)))())
                        P.step("gpsimd", (lambda tt=tt, fc=fc: (lambda e: e.dma_start(
                            out=xprime[tt*128:(tt+1)*128, fc*512:(fc+1)*512], in_=outt[:, 0:512])))(), 16)

        if prologue_od:
            prologue()

        def ln_tile(tp, tt):
            r0 = tp * 1024 + tt * 128
            P.step("sync", (lambda: (lambda e: e.dma_start(out=xt[:], in_=xbase[r0:r0+128, :])))(), 16)
            P.step("vector", lambda e: e.reduce_sum(out=st[:, 0:1], in_=xt[:], axis=AX.X))
            P.step("vector", lambda e: e.tensor_scalar(out=st[:, 0:1], in0=st[:, 0:1], scalar1=1.0/D, scalar2=None, op0=ALU.mult))
            P.step("vector", lambda e: e.tensor_scalar(out=xc[:], in0=xt[:], scalar1=st[:, 0:1], scalar2=None, op0=ALU.subtract))
            P.step("scalar", lambda e: e.activation(out=sq[:], in_=xc[:], func=AF.Square))
            P.step("vector", lambda e: e.reduce_sum(out=st[:, 1:2], in_=sq[:], axis=AX.X))
            P.step("vector", lambda e: e.tensor_scalar(out=st[:, 1:2], in0=st[:, 1:2], scalar1=1.0/D, scalar2=1e-5, op0=ALU.mult, op1=ALU.add))
            P.step("vector", lambda e: e.reciprocal(out=st[:, 2:3], in_=st[:, 1:2]))
            P.step("scalar", lambda e: e.activation(out=st[:, 3:4], in_=st[:, 2:3], func=AF.Sqrt))
            P.step("vector", lambda e: e.tensor_scalar(out=xb[:], in0=xc[:], scalar1=st[:, 3:4], scalar2=None, op0=ALU.mult))
            for gp in range(4):
                for k in range(4):
                    fc = gp * 4 + k
                    P.step("tensor", (lambda fc=fc, k=k: (lambda e: e.transpose(
                        pstb[:, k*128:(k+1)*128], xb[:, fc*128:(fc+1)*128], identb[:])))())
                for k in range(4):
                    fc = gp * 4 + k
                    P.step("vector", (lambda fc=fc, k=k, tt=tt: (lambda e: e.tensor_copy(
                        out=yT[:, fc*1024 + tt*128 : fc*1024 + (tt+1)*128], in_=pstb[:, k*128:(k+1)*128])))())

        def gelu_combine(w):
            P.step("scalar", lambda e: e.activation(out=g1t[:, :w], in_=psg[:, :w], func=AF.Square))
            P.step("vector", lambda e: e.tensor_scalar(out=g1t[:, :w], in0=g1t[:, :w], scalar1=0.044715, scalar2=0.7978845608028654, op0=ALU.mult, op1=ALU.add))
            P.step("vector", lambda e: e.tensor_tensor(out=g2t[:, :w], in0=g1t[:, :w], in1=psg[:, :w], op=ALU.mult))
            P.step("scalar", lambda e: e.activation(out=g2t[:, :w], in_=g2t[:, :w], func=AF.Tanh))
            P.step("vector", lambda e: e.scalar_tensor_tensor(out=g3t[:, :w], in0=g2t[:, :w], scalar=1.0, in1=psg[:, :w], op0=ALU.add, op1=ALU.mult))
            P.step("vector", lambda e: e.scalar_tensor_tensor(out=g3t[:, :w], in0=g3t[:, :w], scalar=0.5, in1=psu[:, :w], op0=ALU.mult, op1=ALU.mult))

        def silu_combine(w):
            P.step("scalar", lambda e: e.activation(out=g1t[:, :w], in_=psg[:, :w], func=AF.Silu))
            P.step("vector", lambda e: e.tensor_tensor(out=g3t[:, :w], in0=g1t[:, :w], in1=psu[:, :w], op=ALU.mult))

        for tp in range(2):
            for tt in range(8):
                ln_tile(tp, tt)
            lastgu = {}
            lastdn = {}
            for s in range(NSTRIP):
                for nt in range(16):
                    slab = s * 16 + nt
                    j = slab
                    wa, wb = wsl[j % 2], wsl2[j % 2]
                    dep = lastgu.get(j - 2)
                    P.wstep("sync", (lambda slab=slab, wa=wa: (lambda e: e.dma_start(
                        out=wa[:], in_=wgb[slab].rearrange("p kc n -> p (kc n)"))))(), dep=dep)
                    wcnt = P.wstep("sync", (lambda slab=slab, wb=wb: (lambda e: e.dma_start(
                        out=wb[:], in_=wub[slab].rearrange("p kc n -> p (kc n)"))))(), dep=dep)
                    for tc2 in range(2):
                        tcs = tc2 * 512
                        P.step("tensor", (lambda tcs=tcs, wa=wa: (lambda e: mm_group(e, psg[:, :],
                            [(wa[:, kc*128:(kc+1)*128], yT[:, kc*1024 + tcs : kc*1024 + tcs + 512]) for kc in range(16)])))(),
                            wwait=(wcnt if tc2 == 0 else None))
                        P.step("tensor", (lambda tcs=tcs, wb=wb: (lambda e: mm_group(e, psu[:, :],
                            [(wb[:, kc*128:(kc+1)*128], yT[:, kc*1024 + tcs : kc*1024 + tcs + 512]) for kc in range(16)])))())
                        if L0:
                            gelu_combine(512)
                        else:
                            silu_combine(512)
                        lastgu[j] = P.step("vector", (lambda nt=nt, tcs=tcs: (lambda e: e.tensor_copy(
                            out=inter[:, nt*1024 + tcs : nt*1024 + tcs + 512], in_=g3t[:, 0:512])))())
                for ht in range(16):
                    dj = s * 16 + ht
                    wd = wdl[dj % 2]
                    P.wstep("sync", (lambda ht=ht, s=s, wd=wd: (lambda e: e.dma_start(
                        out=wd[:], in_=wdb[ht, :, s*16:(s+1)*16, :].rearrange("p kc n -> p (kc n)"))))(),
                        dep=lastdn.get(dj - 2))
                    wcnt = P.wcount
                    for tc2 in range(2):
                        tcs = tc2 * 512
                        P.step("tensor", (lambda tcs=tcs, wd=wd: (lambda e: mm_group(e, psd[:, :],
                            [(wd[:, j2*128:(j2+1)*128], inter[:, j2*1024 + tcs : j2*1024 + tcs + 512]) for j2 in range(16)])))(),
                            wwait=(wcnt if tc2 == 0 else None))
                        if s == 0:
                            lastdn[dj] = P.step("vector", (lambda ht=ht, tcs=tcs: (lambda e: e.tensor_copy(
                                out=acc[:, ht*1024 + tcs : ht*1024 + tcs + 512], in_=psd[:, :])))())
                        else:
                            lastdn[dj] = P.step("vector", (lambda ht=ht, tcs=tcs: (lambda e: e.tensor_add(
                                out=acc[:, ht*1024 + tcs : ht*1024 + tcs + 512],
                                in0=acc[:, ht*1024 + tcs : ht*1024 + tcs + 512], in1=psd[:, :])))())
            for tt in range(8):
                r0 = tp * 1024 + tt * 128
                P.step("sync", (lambda r0=r0: (lambda e: e.dma_start(out=xt[:], in_=xbase[r0:r0+128, :])))(), 16)
                for gp in range(4):
                    for k in range(4):
                        ht = gp * 4 + k
                        P.step("tensor", (lambda ht=ht, k=k, tt=tt: (lambda e: e.transpose(
                            pstf[:, k*128:(k+1)*128], acc[:, ht*1024 + tt*128 : ht*1024 + (tt+1)*128], identf[:])))())
                    P.step("vector", (lambda gp=gp: (lambda e: e.scalar_tensor_tensor(
                        out=outt[:, gp*512:(gp+1)*512], in0=pstf[:, :], scalar=0.5,
                        in1=xt[:, gp*512:(gp+1)*512], op0=ALU.mult, op1=ALU.add)))())
                P.step("gpsimd", (lambda r0=r0: (lambda e: e.dma_start(out=x_out[r0:r0+128, :], in_=outt[:])))(), 16)

        P.emit(block, sem, wsem)
    return nc


SCALE1 = 1.0 / float(np.sqrt(128.0))
CSHIFT = 30.0

def build_attn1():
    nc = bass.Bass("TRN2", target_bir_lowering=False)
    x_in = nc.dram_tensor("x_in", [T, D], F32, kind="ExternalInput").ap()
    wq_in = nc.dram_tensor("wq_in", [D, 2048], F32, kind="ExternalInput").ap()
    wk_in = nc.dram_tensor("wk_in", [D, 512], F32, kind="ExternalInput").ap()
    wv_in = nc.dram_tensor("wv_in", [D, 512], F32, kind="ExternalInput").ap()
    cos_in = nc.dram_tensor("cos_in", [128, 32 * 64], F32, kind="ExternalInput").ap()
    sin_in = nc.dram_tensor("sin_in", [128, 32 * 64], F32, kind="ExternalInput").ap()
    lamv = nc.dram_tensor("lamv", [1, 512], F32, kind="ExternalInput").ap()
    gnw_in = nc.dram_tensor("gnw_in", [128, 16], F32, kind="ExternalInput").ap()
    gnb_in = nc.dram_tensor("gnb_in", [128, 16], F32, kind="ExternalInput").ap()
    dmask_in = nc.dram_tensor("dmask_in", [128, 4 * 512], F32, kind="ExternalInput").ap()
    ident_in = nc.dram_tensor("ident_in", [128, 128], BF16, kind="ExternalInput").ap()
    oT_out = nc.dram_tensor("oT_out", [2048, T], BF16, kind="ExternalOutput").ap()
    xhT = nc.dram_tensor("xhT", [D, T], BF16, kind="Internal").ap()

    with ExitStack() as stx:
        cbuf = _sb(nc, stx, "cbuf", [128, 8], F32)
        xt = _sb(nc, stx, "xt", [128, 2048], F32)
        xc = _sb(nc, stx, "xc", [128, 2048], F32)
        sq = _sb(nc, stx, "sq", [128, 2048], F32)
        st = _sb(nc, stx, "st", [128, 8], F32)
        xb = _sb(nc, stx, "xb", [128, 2048], BF16)
        xq = _sb(nc, stx, "xq", [128, 2048], BF16)
        identb = _sb(nc, stx, "ident", [128, 128], BF16)
        cossb = _sb(nc, stx, "cossb", [128, 2048], F32)
        sinsb = _sb(nc, stx, "sinsb", [128, 2048], F32)
        wkb = _sb(nc, stx, "wkb", [128, 16 * 512], BF16)
        wvb = _sb(nc, stx, "wvb", [128, 16 * 512], BF16)
        wqb = _sb(nc, stx, "wqb", [128, 16 * 256], BF16)
        KT = _sb(nc, stx, "KT", [128, 4 * T], BF16)
        Vb = _sb(nc, stx, "Vb", [128, 32 * 512], BF16)
        QT = wkb
        E1 = _sb(nc, stx, "E1", [128, T], BF16)
        E2 = _sb(nc, stx, "E2", [128, T], BF16)
        Pb = _sb(nc, stx, "Pb", [128, T], BF16)
        PT = _sb(nc, stx, "PT", [128, T], BF16)
        oh = wvb
        dmsk = _sb(nc, stx, "dmsk", [128, 4 * 512], F32)
        ktmp = _sb(nc, stx, "ktmp", [128, 512], F32)
        ktb = _sb(nc, stx, "ktb", [128, 512], BF16)
        sct = _sb(nc, stx, "sct", [128, 512], F32)
        gnw = _sb(nc, stx, "gnw", [128, 16], F32)
        gnb = _sb(nc, stx, "gnb", [128, 16], F32)
        lam_s = _sb(nc, stx, "lam_s", [128, 16], F32)
        ones1 = _sb(nc, stx, "ones1", [1, 128], F32)
        onescol = _sb(nc, stx, "onescol", [128, 1], F32)
        gst = _sb(nc, stx, "gst", [128, 8], F32)
        sc1 = _sb(nc, stx, "sc1", [1, 8], F32)
        psA = _ps(nc, stx, "psA", [128, 512], F32)
        psB = _ps(nc, stx, "psB", [128, 512], F32)
        psT = _ps(nc, stx, "psT", [128, 512], BF16)
        psO1 = _ps(nc, stx, "psO1", [128, 512], F32)
        psO2 = _ps(nc, stx, "psO2", [128, 512], F32)
        psS = _ps(nc, stx, "psS", [128, 128], F32)
        sem = stx.enter_context(nc.semaphore("sem"))
        block = stx.enter_context(nc.Block())

        P = Prog()
        for s in register_consts(nc, cbuf, [SCALE1, -CSHIFT, 1e-5, 1.0, -1.0, 0.8, 1.0 / (T * 256.0)]):
            P.step(*s)
        P.step("sync", lambda e: e.dma_start(out=identb[:], in_=ident_in[:]), 16)
        P.step("sync", lambda e: e.dma_start(out=cossb[:], in_=cos_in[:]), 16)
        P.step("sync", lambda e: e.dma_start(out=sinsb[:], in_=sin_in[:]), 16)
        P.step("sync", lambda e: e.dma_start(out=dmsk[:], in_=dmask_in[:]), 16)
        P.step("sync", lambda e: e.dma_start(out=gnw[:, :], in_=gnw_in[:]), 16)
        P.step("sync", lambda e: e.dma_start(out=gnb[:, :], in_=gnb_in[:]), 16)
        P.step("vector", lambda e: e.memset(ones1[:], 1.0))
        P.step("vector", lambda e: e.memset(onescol[:], 1.0))
        P.step("vector", lambda e: e.memset(gst[:], 0.0))

        def lam_steps():
            P.step("sync", lambda e: e.dma_start(out=xt[0:1, 0:512], in_=lamv[:, :]), 16)
            P.step("vector", lambda e: e.tensor_mul(out=xc[0:1, 0:128], in0=xt[0:1, 0:128], in1=xt[0:1, 128:256]))
            P.step("vector", lambda e: e.tensor_mul(out=xc[0:1, 128:256], in0=xt[0:1, 256:384], in1=xt[0:1, 384:512]))
            P.step("vector", lambda e: e.reduce_sum(out=sc1[0:1, 0:1], in_=xc[0:1, 0:128], axis=AX.X))
            P.step("vector", lambda e: e.reduce_sum(out=sc1[0:1, 1:2], in_=xc[0:1, 128:256], axis=AX.X))
            P.step("scalar", lambda e: e.activation(out=sc1[0:1, 2:3], in_=sc1[0:1, 0:1], func=AF.Exp))
            P.step("scalar", lambda e: e.activation(out=sc1[0:1, 3:4], in_=sc1[0:1, 1:2], func=AF.Exp))
            P.step("vector", lambda e: e.tensor_sub(out=sc1[0:1, 4:5], in0=sc1[0:1, 3:4], in1=sc1[0:1, 2:3]))
            P.step("vector", lambda e: e.tensor_scalar(out=sc1[0:1, 4:5], in0=sc1[0:1, 4:5], scalar1=1.0, scalar2=-0.2, op0=ALU.mult, op1=ALU.add))
            P.step("tensor", lambda e: e.matmul(psS[:, 0:1], ones1[:, :], sc1[0:1, 4:5], start=True, stop=True))
            P.step("vector", lambda e: e.tensor_copy(out=lam_s[:, 8:9], in_=psS[:, 0:1]))
        lam_steps()

        def loadcast(w_in, dst, cols):
            for fc in range(16):
                P.step("sync", (lambda fc=fc: (lambda e: e.dma_start(out=xt[:, 0:cols], in_=w_in[fc*128:(fc+1)*128, :])))(), 16)
                P.step("vector", (lambda fc=fc: (lambda e: e.tensor_copy(out=dst[:, fc*cols:(fc+1)*cols], in_=xt[:, 0:cols])))())
        loadcast(wk_in, wkb, 512)
        loadcast(wv_in, wvb, 512)

        def rope_block(src_ap, dst_ap, tt, tmp):
            cs = cossb[:, tt*64:(tt+1)*64]
            sn = sinsb[:, tt*64:(tt+1)*64]
            x1 = src_ap[:, 0:128:2]; x2 = src_ap[:, 1:128:2]
            t1 = tmp[:, 0:64]; t2 = tmp[:, 64:128]
            P.step("vector", lambda e: e.tensor_tensor(out=t1, in0=x1, in1=cs, op=ALU.mult))
            P.step("vector", lambda e: e.scalar_tensor_tensor(out=t2, in0=x2, scalar=-1.0, in1=sn, op0=ALU.mult, op1=ALU.mult))
            P.step("vector", lambda e: e.tensor_add(out=dst_ap[:, 0:128:2], in0=t1, in1=t2))
            P.step("vector", lambda e: e.tensor_tensor(out=t1, in0=x1, in1=sn, op=ALU.mult))
            P.step("vector", lambda e: e.tensor_tensor(out=t2, in0=x2, in1=cs, op=ALU.mult))
            P.step("vector", lambda e: e.tensor_add(out=dst_ap[:, 1:128:2], in0=t1, in1=t2))

        for tt in range(32):
            r0 = tt * 128
            P.step("sync", (lambda r0=r0: (lambda e: e.dma_start(out=xt[:], in_=x_in[r0:r0+128, :])))(), 16)
            P.step("vector", lambda e: e.reduce_sum(out=st[:, 0:1], in_=xt[:], axis=AX.X))
            P.step("vector", lambda e: e.tensor_scalar(out=st[:, 0:1], in0=st[:, 0:1], scalar1=1.0/D, scalar2=None, op0=ALU.mult))
            P.step("vector", lambda e: e.tensor_scalar(out=xc[:], in0=xt[:], scalar1=st[:, 0:1], scalar2=None, op0=ALU.subtract))
            P.step("scalar", lambda e: e.activation(out=sq[:], in_=xc[:], func=AF.Square))
            P.step("vector", lambda e: e.reduce_sum(out=st[:, 1:2], in_=sq[:], axis=AX.X))
            P.step("vector", lambda e: e.tensor_scalar(out=st[:, 1:2], in0=st[:, 1:2], scalar1=1.0/D, scalar2=1e-5, op0=ALU.mult, op1=ALU.add))
            P.step("vector", lambda e: e.reciprocal(out=st[:, 2:3], in_=st[:, 1:2]))
            P.step("scalar", lambda e: e.activation(out=st[:, 3:4], in_=st[:, 2:3], func=AF.Sqrt))
            P.step("vector", lambda e: e.tensor_scalar(out=xb[:], in0=xc[:], scalar1=st[:, 3:4], scalar2=None, op0=ALU.mult))
            for gp in range(4):
                for k in range(4):
                    fc = gp * 4 + k
                    P.step("tensor", (lambda fc=fc, k=k: (lambda e: e.transpose(
                        psT[:, k*128:(k+1)*128], xb[:, fc*128:(fc+1)*128], identb[:])))())
                for k in range(4):
                    fc = gp * 4 + k
                    P.step("vector", (lambda fc=fc, k=k: (lambda e: e.tensor_copy(
                        out=xq[:, fc*128:(fc+1)*128], in_=psT[:, k*128:(k+1)*128])))())
            P.step("gpsimd", (lambda tt=tt: (lambda e: e.dma_start(
                out=xhT.rearrange("(fc p) t -> p fc t", p=128)[:, :, tt*128:(tt+1)*128],
                in_=xq[:].rearrange("p (fc t) -> p fc t", t=128))))(), 16)
            P.step("tensor", lambda e: mm_group(e, psA[:, :],
                [(xq[:, fc*128:(fc+1)*128], wkb[:, fc*512:(fc+1)*512]) for fc in range(16)]))
            P.step("tensor", lambda e: mm_group(e, psB[:, :],
                [(xq[:, fc*128:(fc+1)*128], wvb[:, fc*512:(fc+1)*512]) for fc in range(16)]))
            for blk in range(4):
                rope_block(psA[:, blk*128:(blk+1)*128], ktb[:, blk*128:(blk+1)*128], tt, ktmp)
            for blk in range(4):
                P.step("tensor", (lambda blk=blk: (lambda e: e.transpose(
                    psT[:, blk*128:(blk+1)*128], ktb[:, blk*128:(blk+1)*128], identb[:])))())
            for blk in range(4):
                P.step("vector", (lambda blk=blk, tt=tt: (lambda e: e.tensor_copy(
                    out=KT[:, blk*T + tt*128 : blk*T + (tt+1)*128], in_=psT[:, blk*128:(blk+1)*128])))())
            P.step("vector", (lambda tt=tt: (lambda e: e.tensor_copy(out=Vb[:, tt*512:(tt+1)*512], in_=psB[:, :])))())

        for hh in range(8):
            kv = hh // 4
            loadcast(wq_in[:, hh*256:(hh+1)*256], wqb, 256)
            for tt in range(32):
                P.step("sync", (lambda tt=tt: (lambda e: e.dma_start(
                    out=xq[:].rearrange("p (fc t) -> p fc t", t=128),
                    in_=xhT.rearrange("(fc p) t -> p fc t", p=128)[:, :, tt*128:(tt+1)*128])))(), 16)
                P.step("tensor", lambda e: mm_group(e, psA[:, 0:256],
                    [(xq[:, fc*128:(fc+1)*128], wqb[:, fc*256:(fc+1)*256]) for fc in range(16)]))
                for blk in range(2):
                    rope_block(psA[:, blk*128:(blk+1)*128], ktb[:, blk*128:(blk+1)*128], tt, ktmp)
                for blk in range(2):
                    P.step("tensor", (lambda blk=blk: (lambda e: e.transpose(
                        psT[:, blk*128:(blk+1)*128], ktb[:, blk*128:(blk+1)*128], identb[:])))())
                for blk in range(2):
                    P.step("vector", (lambda blk=blk, tt=tt: (lambda e: e.tensor_copy(
                        out=QT[:, blk*T + tt*128 : blk*T + (tt+1)*128], in_=psT[:, blk*128:(blk+1)*128])))())
            for tt in range(32):
                nsc = tt // 4 + 1
                for half, Ebuf, kbi in ((0, E1, 2*kv), (1, E2, 2*kv+1)):
                    for sc in range(nsc):
                        diag = (sc == nsc - 1)
                        P.step("tensor", (lambda half=half, kbi=kbi, tt=tt, sc=sc: (lambda e: e.matmul(
                            psA[:, :], QT[:, half*T + tt*128 : half*T + (tt+1)*128],
                            KT[:, kbi*T + sc*512 : kbi*T + (sc+1)*512], start=True, stop=True)))())
                        if diag:
                            P.step("vector", (lambda tt=tt: (lambda e: e.tensor_add(
                                out=sct[:, :], in0=psA[:, :], in1=dmsk[:, (tt % 4)*512:(tt % 4 + 1)*512])))())
                            src = sct
                        else:
                            src = psA
                        P.step("scalar", (lambda src=src, Ebuf=Ebuf, sc=sc, half=half: (lambda e: e.activation(
                            out=Ebuf[:, sc*512:(sc+1)*512], in_=src[:, :], func=AF.Exp,
                            scale=SCALE1, bias=-CSHIFT,
                            accum_out=st[:, 4+half:5+half] if sc == 0 else st[:, 6:7])))())
                        if sc != 0:
                            P.step("vector", (lambda half=half: (lambda e: e.tensor_add(
                                out=st[:, 4+half:5+half], in0=st[:, 4+half:5+half], in1=st[:, 6:7])))())
                P.step("vector", lambda e: e.reciprocal(out=st[:, 4:5], in_=st[:, 4:5]))
                P.step("vector", lambda e: e.reciprocal(out=st[:, 5:6], in_=st[:, 5:6]))
                P.step("vector", lambda e: e.tensor_scalar(out=st[:, 6:7], in0=st[:, 5:6], scalar1=lam_s[:, 8:9], scalar2=None, op0=ALU.mult))
                sw = nsc * 512
                P.step("vector", (lambda sw=sw: (lambda e: e.tensor_scalar(out=Pb[:, 0:sw], in0=E1[:, 0:sw], scalar1=st[:, 4:5], scalar2=None, op0=ALU.mult)))())
                P.step("vector", (lambda sw=sw: (lambda e: e.scalar_tensor_tensor(out=Pb[:, 0:sw], in0=E2[:, 0:sw], scalar=st[:, 6:7], in1=Pb[:, 0:sw], op0=ALU.mult, op1=ALU.add)))())
                nst = nsc * 4
                for g0 in range(0, nst, 4):
                    kmax = min(4, nst - g0)
                    for k in range(kmax):
                        P.step("tensor", (lambda g0=g0, k=k: (lambda e: e.transpose(
                            psT[:, k*128:(k+1)*128], Pb[:, (g0+k)*128:(g0+k+1)*128], identb[:])))())
                    for k in range(kmax):
                        P.step("vector", (lambda g0=g0, k=k: (lambda e: e.tensor_copy(
                            out=PT[:, (g0+k)*128:(g0+k+1)*128], in_=psT[:, k*128:(k+1)*128])))())
                for dc, psO in ((0, psO1), (1, psO2)):
                    P.step("tensor", (lambda dc=dc, psO=psO, nst=nst, tt=tt: (lambda e: mm_group(e, psO[:, 0:128],
                        [(Vb[:, stl*512 + dc*128 : stl*512 + (dc+1)*128], PT[:, stl*128:(stl+1)*128])
                         for stl in range(nst)])))())
                    P.step("vector", (lambda psO=psO: (lambda e: e.reduce_sum(out=st[:, 6:7], in_=psO[:, 0:128], axis=AX.X)))())
                    P.step("vector", lambda e: e.tensor_add(out=gst[:, 0:1], in0=gst[:, 0:1], in1=st[:, 6:7]))
                    P.step("scalar", (lambda psO=psO: (lambda e: e.activation(out=ktmp[:, 0:128], in_=psO[:, 0:128], func=AF.Square, accum_out=st[:, 7:8])))())
                    P.step("vector", lambda e: e.tensor_add(out=gst[:, 1:2], in0=gst[:, 1:2], in1=st[:, 7:8]))
                    P.step("vector", (lambda psO=psO, dc=dc, tt=tt: (lambda e: e.tensor_copy(
                        out=oh[:, dc*T + tt*128 : dc*T + (tt+1)*128], in_=psO[:, 0:128])))())
            P.step("tensor", lambda e: e.matmul(psS[0:1, 0:2], onescol[:, :], gst[:, 0:2], start=True, stop=True))
            P.step("vector", lambda e: e.tensor_scalar(out=sc1[0:1, 0:2], in0=psS[0:1, 0:2], scalar1=1.0/(T*256.0), scalar2=None, op0=ALU.mult))
            P.step("vector", lambda e: e.tensor_mul(out=sc1[0:1, 2:3], in0=sc1[0:1, 0:1], in1=sc1[0:1, 0:1]))
            P.step("vector", lambda e: e.tensor_sub(out=sc1[0:1, 3:4], in0=sc1[0:1, 1:2], in1=sc1[0:1, 2:3]))
            P.step("vector", lambda e: e.tensor_scalar(out=sc1[0:1, 3:4], in0=sc1[0:1, 3:4], scalar1=1.0, scalar2=1e-5, op0=ALU.mult, op1=ALU.add))
            P.step("vector", lambda e: e.reciprocal(out=sc1[0:1, 3:4], in_=sc1[0:1, 3:4]))
            P.step("scalar", lambda e: e.activation(out=sc1[0:1, 3:4], in_=sc1[0:1, 3:4], func=AF.Sqrt))
            P.step("tensor", lambda e: e.matmul(psS[:, 4:5], ones1[:, :], sc1[0:1, 0:1], start=True, stop=True))
            P.step("tensor", lambda e: e.matmul(psS[:, 5:6], ones1[:, :], sc1[0:1, 3:4], start=True, stop=True))
            P.step("vector", lambda e: e.tensor_copy(out=lam_s[:, 10:11], in_=psS[:, 4:5]))
            P.step("vector", lambda e: e.tensor_copy(out=lam_s[:, 11:12], in_=psS[:, 5:6]))
            for dc in range(2):
                gw = gnw[:, hh*2+dc : hh*2+dc+1]
                gb = gnb[:, hh*2+dc : hh*2+dc+1]
                P.step("vector", (lambda gw=gw: (lambda e: e.tensor_tensor(out=st[:, 4:5], in0=gw, in1=lam_s[:, 11:12], op=ALU.mult)))())
                P.step("vector", lambda e: e.tensor_scalar(out=st[:, 4:5], in0=st[:, 4:5], scalar1=0.8, scalar2=None, op0=ALU.mult))
                P.step("vector", (lambda gb=gb: (lambda e: e.tensor_tensor(out=st[:, 5:6], in0=lam_s[:, 10:11], in1=st[:, 4:5], op=ALU.mult)))())
                P.step("vector", (lambda gb=gb: (lambda e: e.scalar_tensor_tensor(out=st[:, 5:6], in0=gb, scalar=0.8, in1=st[:, 5:6], op0=ALU.mult, op1=ALU.subtract)))())
                P.step("vector", (lambda dc=dc, hh=hh: (lambda e: e.tensor_scalar(
                    out=oh[:, dc*T:(dc+1)*T], in0=oh[:, dc*T:(dc+1)*T],
                    scalar1=st[:, 4:5], scalar2=st[:, 5:6], op0=ALU.mult, op1=ALU.add)))())
                P.step("gpsimd", (lambda dc=dc, hh=hh: (lambda e: e.dma_start(
                    out=oT_out[hh*256 + dc*128 : hh*256 + (dc+1)*128, :], in_=oh[:, dc*T:(dc+1)*T])))(), 16)
            P.step("vector", lambda e: e.memset(gst[:], 0.0))

        P.emit(block, sem)
    return nc


def run_attn1(x2d, I):
    nc = _get("attn1", build_attn1)
    Wq = np.asarray(I["L1_attn_Wq"], dtype=np.float32)
    Wk = np.asarray(I["L1_attn_Wk"], dtype=np.float32)
    Wv = np.asarray(I["L1_attn_Wv"], dtype=np.float32)
    gnw = np.asarray(I["L1_attn_gn_w"], dtype=np.float32)
    gnb = np.asarray(I["L1_attn_gn_b"], dtype=np.float32)
    lamv = np.concatenate([np.asarray(I[k], dtype=np.float32) for k in
                           ("L1_attn_lambda_q1", "L1_attn_lambda_k1", "L1_attn_lambda_q2", "L1_attn_lambda_k2")])[None, :]
    inv = 1.0 / (10000.0 ** (np.arange(0, 128, 2, dtype=np.float32) / 128.0))
    freqs = np.arange(T, dtype=np.float32)[:, None] * inv[None, :]
    cos = np.cos(freqs).reshape(32, 128, 64).transpose(1, 0, 2).reshape(128, 2048)
    sin = np.sin(freqs).reshape(32, 128, 64).transpose(1, 0, 2).reshape(128, 2048)
    dmask = np.zeros((4, 128, 512), dtype=np.float32)
    for j in range(4):
        tl = np.arange(128)[:, None]; sl = np.arange(512)[None, :]
        dmask[j] = np.where(sl <= tl + j*128, 0.0, -1e9)
    dmask = np.ascontiguousarray(dmask.transpose(1, 0, 2).reshape(128, 4*512))
    identb_np = np.eye(128, dtype=BF16NP)
    in_maps = []
    for c in range(NCORES):
        b, g = c // 2, c % 2
        hs = slice(g*8*256, (g+1)*8*256)
        ks = slice(g*2*256, (g+1)*2*256)
        in_maps.append({
            "x_in": np.ascontiguousarray(x2d[b*T:(b+1)*T]),
            "wq_in": np.ascontiguousarray(Wq[:, hs]),
            "wk_in": np.ascontiguousarray(Wk[:, ks]),
            "wv_in": np.ascontiguousarray(Wv[:, ks]),
            "cos_in": cos, "sin_in": sin, "lamv": lamv,
            "gnw_in": np.ascontiguousarray(gnw[hs].reshape(16, 128).T),
            "gnb_in": np.ascontiguousarray(gnb[hs].reshape(16, 128).T),
            "dmask_in": dmask, "ident_in": identb_np,
        })
    res = run_bass_kernel_spmd(nc, in_maps, core_ids=list(range(NCORES)))
    oT = np.zeros((4096, TOK), dtype=BF16NP)
    for c in range(NCORES):
        b, g = c // 2, c % 2
        oT[g*2048:(g+1)*2048, b*T:(b+1)*T] = res.results[c]["oT_out"]
    return oT


SCALE0 = 1.0 / float(np.sqrt(128.0))
DEBUG_ATTN0 = False

def build_attn0():
    nc = bass.Bass("TRN2", target_bir_lowering=False)
    x_in = nc.dram_tensor("x_in", [T, D], F32, kind="ExternalInput").ap()
    wq_in = nc.dram_tensor("wq_in", [D, 1024], F32, kind="ExternalInput").ap()
    wk_in = nc.dram_tensor("wk_in", [D, 1024], F32, kind="ExternalInput").ap()
    wv_in = nc.dram_tensor("wv_in", [D, 1024], F32, kind="ExternalInput").ap()
    ww_in = nc.dram_tensor("ww_in", [D, 1024], F32, kind="ExternalInput").ap()
    wf1_in = nc.dram_tensor("wf1_in", [D, 128], F32, kind="ExternalInput").ap()
    wf2_in = nc.dram_tensor("wf2_in", [128, 1024], F32, kind="ExternalInput").ap()
    wb_in = nc.dram_tensor("wb_in", [D, 8], F32, kind="ExternalInput").ap()
    wg1_in = nc.dram_tensor("wg1_in", [D, 128], F32, kind="ExternalInput").ap()
    wg2_in = nc.dram_tensor("wg2_in", [128, 1024], F32, kind="ExternalInput").ap()
    bg2_in = nc.dram_tensor("bg2_in", [1, 1024], F32, kind="ExternalInput").ap()
    cq_in = nc.dram_tensor("cq_in", [128, 32], F32, kind="ExternalInput").ap()
    ck_in = nc.dram_tensor("ck_in", [128, 32], F32, kind="ExternalInput").ap()
    cv_in = nc.dram_tensor("cv_in", [128, 32], F32, kind="ExternalInput").ap()
    normw_in = nc.dram_tensor("normw_in", [1, 128], F32, kind="ExternalInput").ap()
    ut1_in = nc.dram_tensor("ut1_in", [64, 64], BF16, kind="ExternalInput").ap()
    mask64_in = nc.dram_tensor("mask64_in", [64, 64], F32, kind="ExternalInput").ap()
    ident_in = nc.dram_tensor("ident_in", [128, 128], BF16, kind="ExternalInput").ap()
    identf_in = nc.dram_tensor("identf_in", [128, 128], F32, kind="ExternalInput").ap()
    o_out = nc.dram_tensor("o_out", [T, 1024], BF16, kind="ExternalOutput").ap()
    _ikind = "ExternalOutput" if DEBUG_ATTN0 else "Internal"
    xhT = nc.dram_tensor("xhT", [D, T], BF16, kind=_ikind).ap()
    qd = nc.dram_tensor("qd", [1024, T], BF16, kind=_ikind).ap()
    kd = nc.dram_tensor("kd", [1024, T], BF16, kind=_ikind).ap()
    vd = nc.dram_tensor("vd", [1024, T], BF16, kind=_ikind).ap()
    wdd = nc.dram_tensor("wdd", [1024, T], BF16, kind=_ikind).ap()
    gd = nc.dram_tensor("gd", [T, 1024], BF16, kind=_ikind).ap()
    sgd = nc.dram_tensor("sgd", [T, 1024], BF16, kind=_ikind).ap()
    betad = nc.dram_tensor("betad", [T, 8], F32, kind=_ikind).ap()

    with ExitStack() as stx:
        cbuf = _sb(nc, stx, "cbuf", [128, 8], F32)
        arena = _sb(nc, stx, "arena", [128, 24576], BF16)
        xt = _sb(nc, stx, "xt", [128, 2048], F32)
        xc = _sb(nc, stx, "xc", [128, 2048], F32)
        sq = _sb(nc, stx, "sq", [128, 2048], F32)
        st = _sb(nc, stx, "st", [128, 8], F32)
        xb = _sb(nc, stx, "xb", [128, 2048], BF16)
        xsT = _sb(nc, stx, "xsT", [128, 2048], BF16)
        identb = _sb(nc, stx, "identb", [128, 128], BF16)
        prebuf = _sb(nc, stx, "prebuf", [128, 520], F32)
        halo = _sb(nc, stx, "halo", [128, 4 * 8 * 4], F32)
        convb = _sb(nc, stx, "convb", [128, 512], F32)
        convo = _sb(nc, stx, "convo", [128, 512], BF16)
        cqb = _sb(nc, stx, "cqb", [128, 32], F32)
        ckb = _sb(nc, stx, "ckb", [128, 32], F32)
        cvb = _sb(nc, stx, "cvb", [128, 32], F32)
        wf1b = _sb(nc, stx, "wf1b", [128, 2048], BF16)
        wf2b = _sb(nc, stx, "wf2b", [128, 1024], BF16)
        wg1b = _sb(nc, stx, "wg1b", [128, 2048], BF16)
        wg2b = _sb(nc, stx, "wg2b", [128, 1024], BF16)
        wbb = _sb(nc, stx, "wbb", [128, 128], BF16)
        bg2bc = _sb(nc, stx, "bg2bc", [128, 1024], F32)
        bg2s = _sb(nc, stx, "bg2s", [1, 1024], F32)
        y1b = _sb(nc, stx, "y1b", [128, 512], BF16)
        gout = _sb(nc, stx, "gout", [128, 4096], BF16)
        sgout = _sb(nc, stx, "sgout", [128, 4096], BF16)
        gtmp = _sb(nc, stx, "gtmp", [128, 512], F32)
        gtmp2 = _sb(nc, stx, "gtmp2", [128, 512], F32)
        ones1 = _sb(nc, stx, "ones1", [1, 128], F32)
        onescol = _sb(nc, stx, "onescol", [128, 1], F32)
        nwrow = _sb(nc, stx, "nwrow", [1, 128], F32)
        ut1 = _sb(nc, stx, "ut1", [64, 64], BF16)
        mask64 = _sb(nc, stx, "mask64", [64, 64], F32)
        nwb = _sb(nc, stx, "nwb", [128, 128], F32)
        identf2 = _sb(nc, stx, "identf2", [128, 128], F32)
        gch2 = _sb(nc, stx, "gch2", [64, 8 * 1024], BF16)
        sgch2 = _sb(nc, stx, "sgch2", [64, 8 * 1024], BF16)
        obuf2 = _sb(nc, stx, "obuf2", [64, 8 * 1024], BF16)
        bsc = _sb(nc, stx, "bsc", [64, 64], F32)
        wtf = _sb(nc, stx, "wtf", [128, 512], F32)
        wfl = _sb(nc, stx, "wfl", [128, 512], F32)
        SkT = _sb(nc, stx, "SkT", [128, 8 * 128], F32)
        Svs = _sb(nc, stx, "Svs", [128, 8 * 128], F32)
        Sk0b = _sb(nc, stx, "Sk0b", [128, 8 * 128], BF16)
        Sv0b = _sb(nc, stx, "Sv0b", [128, 8 * 128], BF16)
        lamb = _sb(nc, stx, "lamb", [128, 64], F32)
        lamT = _sb(nc, stx, "lamT", [64, 128], F32)
        rlam = _sb(nc, stx, "rlam", [128, 64], F32)
        rlam2 = _sb(nc, stx, "rlam2", [128, 64], F32)
        PTb = _sb(nc, stx, "PTb", [128, 64], BF16)
        PTpb = _sb(nc, stx, "PTpb", [128, 64], BF16)
        Pb2 = _sb(nc, stx, "Pb2", [64, 128], BF16)
        Ppb = _sb(nc, stx, "Ppb", [64, 128], BF16)
        Kb2 = _sb(nc, stx, "Kb2", [64, 128], BF16)
        Vb2 = _sb(nc, stx, "Vb2", [64, 128], BF16)
        ATb = _sb(nc, stx, "ATb", [64, 64], BF16)
        AtTb = _sb(nc, stx, "AtTb", [64, 64], BF16)
        aTb = _sb(nc, stx, "aTb", [128, 64], BF16)
        ab2 = _sb(nc, stx, "ab2", [64, 128], BF16)
        Ea = _sb(nc, stx, "Ea", [64, 128], F32)
        ot2 = _sb(nc, stx, "ot2", [64, 128], F32)
        bankF1 = _ps(nc, stx, "bankF1", [128, 512], F32)
        bankF2 = _ps(nc, stx, "bankF2", [128, 512], F32)
        bankB1 = _ps(nc, stx, "bankB1", [128, 512], BF16)
        bankB2 = _ps(nc, stx, "bankB2", [128, 128], BF16)
        sem = stx.enter_context(nc.semaphore("sem"))
        block = stx.enter_context(nc.Block())

        wpb = arena[:, 0:16384]
        xqc = arena[:, 16384:24576]
        qc = arena[:, 0:4096]
        kc = arena[:, 4096:8192]
        vc = arena[:, 8192:12288]
        wc = arena[:, 12288:16384]

        P = Prog()
        for s in register_consts(nc, cbuf, [SCALE0, -1.0, 1.0]):
            P.step(*s)
        P.step("sync", lambda e: e.dma_start(out=identb[:], in_=ident_in[:]), 16)
        P.step("sync", lambda e: e.dma_start(out=identf2[:], in_=identf_in[:]), 16)
        P.step("sync", lambda e: e.dma_start(out=ut1[:], in_=ut1_in[:]), 16)
        P.step("sync", lambda e: e.dma_start(out=mask64[:], in_=mask64_in[:]), 16)
        P.step("sync", lambda e: e.dma_start(out=cqb[:], in_=cq_in[:]), 16)
        P.step("sync", lambda e: e.dma_start(out=ckb[:], in_=ck_in[:]), 16)
        P.step("sync", lambda e: e.dma_start(out=cvb[:], in_=cv_in[:]), 16)
        P.step("sync", lambda e: e.dma_start(out=bg2s[:], in_=bg2_in[:]), 16)
        P.step("sync", lambda e: e.dma_start(out=nwrow[:], in_=normw_in[:]), 16)
        P.step("vector", lambda e: e.memset(ones1[:], 1.0))
        P.step("vector", lambda e: e.memset(onescol[:], 1.0))
        P.step("vector", lambda e: e.memset(halo[:], 0.0))
        P.step("vector", lambda e: e.memset(SkT[:], 0.0))
        P.step("vector", lambda e: e.memset(Svs[:], 0.0))
        P.step("vector", lambda e: e.memset(Sk0b[:], 0.0))
        P.step("vector", lambda e: e.memset(Sv0b[:], 0.0))
        for cc in range(2):
            P.step("tensor", (lambda cc=cc: (lambda e: e.matmul(bankF1[:, 0:512], ones1[:, :], bg2s[0:1, cc*512:(cc+1)*512], start=True, stop=True)))())
            P.step("vector", (lambda cc=cc: (lambda e: e.tensor_copy(out=bg2bc[:, cc*512:(cc+1)*512], in_=bankF1[:, 0:512])))())
        P.step("tensor", lambda e: e.matmul(bankF1[:, 0:128], ones1[:, :], nwrow[0:1, :], start=True, stop=True))
        P.step("vector", lambda e: e.tensor_copy(out=nwb[:], in_=bankF1[:, 0:128]))

        for tt in range(32):
            r0 = tt * 128
            P.step("sync", (lambda r0=r0: (lambda e: e.dma_start(out=xt[:], in_=x_in[r0:r0+128, :])))(), 16)
            P.step("vector", lambda e: e.reduce_sum(out=st[:, 0:1], in_=xt[:], axis=AX.X))
            P.step("vector", lambda e: e.tensor_scalar(out=st[:, 0:1], in0=st[:, 0:1], scalar1=1.0/D, scalar2=None, op0=ALU.mult))
            P.step("vector", lambda e: e.tensor_scalar(out=xc[:], in0=xt[:], scalar1=st[:, 0:1], scalar2=None, op0=ALU.subtract))
            P.step("scalar", lambda e: e.activation(out=sq[:], in_=xc[:], func=AF.Square))
            P.step("vector", lambda e: e.reduce_sum(out=st[:, 1:2], in_=sq[:], axis=AX.X))
            P.step("vector", lambda e: e.tensor_scalar(out=st[:, 1:2], in0=st[:, 1:2], scalar1=1.0/D, scalar2=1e-5, op0=ALU.mult, op1=ALU.add))
            P.step("vector", lambda e: e.reciprocal(out=st[:, 2:3], in_=st[:, 1:2]))
            P.step("scalar", lambda e: e.activation(out=st[:, 3:4], in_=st[:, 2:3], func=AF.Sqrt))
            P.step("vector", lambda e: e.tensor_scalar(out=xb[:], in0=xc[:], scalar1=st[:, 3:4], scalar2=None, op0=ALU.mult))
            for gp in range(4):
                for k in range(4):
                    fc = gp * 4 + k
                    P.step("tensor", (lambda fc=fc, k=k: (lambda e: e.transpose(
                        bankB1[:, k*128:(k+1)*128], xb[:, fc*128:(fc+1)*128], identb[:])))())
                for k in range(4):
                    fc = gp * 4 + k
                    P.step("vector", (lambda fc=fc, k=k: (lambda e: e.tensor_copy(
                        out=xsT[:, fc*128:(fc+1)*128], in_=bankB1[:, k*128:(k+1)*128])))())
            P.step("gpsimd", (lambda tt=tt: (lambda e: e.dma_start(
                out=xhT.rearrange("(fc p) t -> p fc t", p=128)[:, :, tt*128:(tt+1)*128],
                in_=xsT[:].rearrange("p (fc t) -> p fc t", t=128))))(), 16)

        for pi, (w_in, cw, out_d) in enumerate([(wq_in, cqb, qd), (wk_in, ckb, kd), (wv_in, cvb, vd), (ww_in, cvb, wdd)]):
            for fc in range(16):
                P.step("sync", (lambda w_in=w_in, fc=fc: (lambda e: e.dma_start(out=xt[:, 0:1024], in_=w_in[fc*128:(fc+1)*128, :])))(), 16)
                P.step("vector", (lambda fc=fc: (lambda e: e.tensor_copy(out=wpb[:, fc*1024:(fc+1)*1024], in_=xt[:, 0:1024])))())
            for tc in range(8):
                P.step("sync", (lambda tc=tc: (lambda e: e.dma_start(
                    out=xqc.rearrange("p (fc t) -> p fc t", t=512),
                    in_=xhT.rearrange("(fc p) t -> p fc t", p=128)[:, :, tc*512:(tc+1)*512])))(), 16)
                for ct in range(8):
                    P.step("tensor", (lambda ct=ct: (lambda e: mm_group(e, bankF1[:, 0:512],
                        [(wpb[:, fc*1024 + ct*128 : fc*1024 + (ct+1)*128], xqc[:, fc*512:(fc+1)*512]) for fc in range(16)])))())
                    P.step("vector", lambda e: e.tensor_copy(out=prebuf[:, 3:515], in_=bankF1[:, 0:512]))
                    hs0 = (pi*8 + ct) * 4
                    P.step("vector", (lambda hs0=hs0: (lambda e: e.tensor_copy(out=prebuf[:, 0:3], in_=halo[:, hs0:hs0+3])))())
                    P.step("vector", (lambda cw=cw, ct=ct: (lambda e: e.tensor_scalar(
                        out=convb[:], in0=prebuf[:, 0:512], scalar1=cw[:, ct*4+0:ct*4+1], scalar2=None, op0=ALU.mult)))())
                    for i in range(1, 4):
                        P.step("vector", (lambda cw=cw, ct=ct, i=i: (lambda e: e.scalar_tensor_tensor(
                            out=convb[:], in0=prebuf[:, i:512+i], scalar=cw[:, ct*4+i:ct*4+i+1], in1=convb[:],
                            op0=ALU.mult, op1=ALU.add)))())
                    P.step("scalar", lambda e: e.activation(out=convo[:], in_=convb[:], func=AF.Silu))
                    P.step("vector", (lambda hs0=hs0: (lambda e: e.tensor_copy(out=halo[:, hs0:hs0+3], in_=prebuf[:, 512:515])))())
                    P.step("gpsimd", (lambda out_d=out_d, ct=ct, tc=tc: (lambda e: e.dma_start(
                        out=out_d[ct*128:(ct+1)*128, tc*512:(tc+1)*512], in_=convo[:])))(), 16)

        for fc in range(16):
            P.step("sync", (lambda fc=fc: (lambda e: e.dma_start(out=xt[:, 0:128], in_=wf1_in[fc*128:(fc+1)*128, :])))(), 16)
            P.step("vector", (lambda fc=fc: (lambda e: e.tensor_copy(out=wf1b[:, fc*128:(fc+1)*128], in_=xt[:, 0:128])))())
            P.step("sync", (lambda fc=fc: (lambda e: e.dma_start(out=xt[:, 0:128], in_=wg1_in[fc*128:(fc+1)*128, :])))(), 16)
            P.step("vector", (lambda fc=fc: (lambda e: e.tensor_copy(out=wg1b[:, fc*128:(fc+1)*128], in_=xt[:, 0:128])))())
            P.step("sync", (lambda fc=fc: (lambda e: e.dma_start(out=xt[:, 0:8], in_=wb_in[fc*128:(fc+1)*128, :])))(), 16)
            P.step("vector", (lambda fc=fc: (lambda e: e.tensor_copy(out=wbb[:, fc*8:(fc+1)*8], in_=xt[:, 0:8])))())
        P.step("sync", lambda e: e.dma_start(out=xt[:, 0:1024], in_=wf2_in[:, :]), 16)
        P.step("vector", lambda e: e.tensor_copy(out=wf2b[:], in_=xt[:, 0:1024]))
        P.step("sync", lambda e: e.dma_start(out=xt[:, 0:1024], in_=wg2_in[:, :]), 16)
        P.step("vector", lambda e: e.tensor_copy(out=wg2b[:], in_=xt[:, 0:1024]))
        for tc in range(8):
            P.step("sync", (lambda tc=tc: (lambda e: e.dma_start(
                out=xqc.rearrange("p (fc t) -> p fc t", t=512),
                in_=xhT.rearrange("(fc p) t -> p fc t", p=128)[:, :, tc*512:(tc+1)*512])))(), 16)
            P.step("tensor", lambda e: mm_group(e, bankF1[:, 0:512],
                [(wf1b[:, fc*128:(fc+1)*128], xqc[:, fc*512:(fc+1)*512]) for fc in range(16)]))
            P.step("vector", lambda e: e.tensor_copy(out=y1b[:], in_=bankF1[:, 0:512]))
            for tti in range(4):
                for cc2 in range(2):
                    P.step("tensor", (lambda tti=tti, cc2=cc2: (lambda e: e.matmul(
                        bankF2[:, 0:512], y1b[:, tti*128:(tti+1)*128], wf2b[:, cc2*512:(cc2+1)*512], start=True, stop=True)))())
                    P.step("vector", lambda e: e.tensor_scalar_min(out=gtmp[:], in0=bankF2[:, 0:512], scalar1=0.0))
                    P.step("scalar", lambda e: e.activation(out=gtmp2[:], in_=bankF2[:, 0:512], func=AF.Abs))
                    P.step("scalar", lambda e: e.activation(out=gtmp2[:], in_=gtmp2[:], func=AF.Exp, scale=-1.0))
                    P.step("scalar", lambda e: e.activation(out=gtmp2[:], in_=gtmp2[:], func=AF.Ln, bias=1.0))
                    P.step("vector", (lambda tti=tti, cc2=cc2: (lambda e: e.tensor_sub(
                        out=gout[:, tti*1024 + cc2*512 : tti*1024 + (cc2+1)*512], in0=gtmp[:], in1=gtmp2[:])))())
            P.step("gpsimd", (lambda tc=tc: (lambda e: e.dma_start(
                out=gd[tc*512:(tc+1)*512, :].rearrange("(tti p) c -> p tti c", p=128),
                in_=gout.rearrange("p (tti c) -> p tti c", c=1024))))(), 16)
            P.step("tensor", lambda e: mm_group(e, bankF1[:, 0:512],
                [(wg1b[:, fc*128:(fc+1)*128], xqc[:, fc*512:(fc+1)*512]) for fc in range(16)]))
            P.step("vector", lambda e: e.tensor_copy(out=y1b[:], in_=bankF1[:, 0:512]))
            for tti in range(4):
                for cc2 in range(2):
                    P.step("tensor", (lambda tti=tti, cc2=cc2: (lambda e: e.matmul(
                        bankF2[:, 0:512], y1b[:, tti*128:(tti+1)*128], wg2b[:, cc2*512:(cc2+1)*512], start=True, stop=True)))())
                    P.step("vector", (lambda cc2=cc2: (lambda e: e.tensor_add(
                        out=gtmp[:], in0=bankF2[:, 0:512], in1=bg2bc[:, cc2*512:(cc2+1)*512])))())
                    P.step("scalar", (lambda tti=tti, cc2=cc2: (lambda e: e.activation(
                        out=sgout[:, tti*1024 + cc2*512 : tti*1024 + (cc2+1)*512], in_=gtmp[:], func=AF.Sigmoid)))())
            P.step("gpsimd", (lambda tc=tc: (lambda e: e.dma_start(
                out=sgd[tc*512:(tc+1)*512, :].rearrange("(tti p) c -> p tti c", p=128),
                in_=sgout.rearrange("p (tti c) -> p tti c", c=1024))))(), 16)
            for tti in range(4):
                P.step("tensor", (lambda tti=tti: (lambda e: mm_group(e, bankF2[:, 0:8],
                    [(xqc[:, fc*512 + tti*128 : fc*512 + (tti+1)*128], wbb[:, fc*8:(fc+1)*8]) for fc in range(16)])))())
                P.step("scalar", lambda e: e.activation(out=gtmp[:, 0:8], in_=bankF2[:, 0:8], func=AF.Sigmoid))
                P.step("gpsimd", (lambda tc=tc, tti=tti: (lambda e: e.dma_start(
                    out=betad[tc*512 + tti*128 : tc*512 + (tti+1)*128, :], in_=gtmp[:, 0:8])))(), 16)

        for tc in range(8):
            for src_d, dst in ((qd, qc), (kd, kc), (vd, vc), (wdd, wc)):
                P.step("sync", (lambda src_d=src_d, dst=dst, tc=tc: (lambda e: e.dma_start(
                    out=dst.rearrange("p (ct t) -> p ct t", t=512),
                    in_=src_d.rearrange("(ct p) t -> p ct t", p=128)[:, :, tc*512:(tc+1)*512])))(), 16)
            P.step("sync", (lambda tc=tc: (lambda e: e.dma_start(
                out=gch2.rearrange("p (u c) -> p u c", c=1024),
                in_=gd[tc*512:(tc+1)*512, :].rearrange("(u p) c -> p u c", p=64))))(), 16)
            P.step("sync", (lambda tc=tc: (lambda e: e.dma_start(
                out=sgch2.rearrange("p (u c) -> p u c", c=1024),
                in_=sgd[tc*512:(tc+1)*512, :].rearrange("(u p) c -> p u c", p=64))))(), 16)
            P.step("sync", (lambda tc=tc: (lambda e: e.dma_start(
                out=bsc.rearrange("p (u h) -> p u h", h=8),
                in_=betad[tc*512:(tc+1)*512, :].rearrange("(u p) h -> p u h", p=64))))(), 16)
            for hh in range(8):
                hsl = lambda buf: buf[:, hh*512:(hh+1)*512]
                P.step("vector", (lambda hh=hh: (lambda e: e.tensor_copy(out=wfl[:], in_=wc[:, hh*512:(hh+1)*512])))())
                P.step("scalar", lambda e: e.activation(out=convb[:], in_=wfl[:], func=AF.Square))
                P.step("tensor", lambda e: e.matmul(bankF2[0:1, 0:512], onescol[:, :], convb[:], start=True, stop=True))
                P.step("vector", lambda e: e.tensor_scalar(out=gtmp[0:1, 0:512], in0=bankF2[0:1, 0:512], scalar1=1.0, scalar2=1e-6, op0=ALU.mult, op1=ALU.add))
                P.step("vector", lambda e: e.reciprocal(out=gtmp2[0:1, 0:512], in_=gtmp[0:1, 0:512]))
                P.step("scalar", lambda e: e.activation(out=gtmp2[0:1, 0:512], in_=gtmp2[0:1, 0:512], func=AF.Sqrt))
                P.step("tensor", lambda e: e.matmul(bankF2[:, 0:512], ones1[:, :], gtmp2[0:1, 0:512], start=True, stop=True))
                P.step("vector", lambda e: e.tensor_tensor(out=wtf[:], in0=wfl[:], in1=bankF2[:, 0:512], op=ALU.mult))
                for u in range(8):
                    qsl = qc[:, hh*512 + u*64 : hh*512 + (u+1)*64]
                    ksl = kc[:, hh*512 + u*64 : hh*512 + (u+1)*64]
                    vsl = vc[:, hh*512 + u*64 : hh*512 + (u+1)*64]
                    gsl = gch2[0:64, u*1024 + hh*128 : u*1024 + (hh+1)*128]
                    sgsl = sgch2[0:64, u*1024 + hh*128 : u*1024 + (hh+1)*128]
                    bsl = bsc[0:64, u*8 + hh : u*8 + hh + 1]
                    sk_s = SkT[:, hh*128:(hh+1)*128]
                    sv_s = Svs[:, hh*128:(hh+1)*128]
                    sk0_s = Sk0b[:, hh*128:(hh+1)*128]
                    sv0_s = Sv0b[:, hh*128:(hh+1)*128]
                    P.step("tensor", (lambda gsl=gsl: (lambda e: e.matmul(bankF1[:, 0:64], gsl, ut1[:, :], start=True, stop=True)))())
                    P.step("tensor", (lambda gsl=gsl: (lambda e: e.matmul(bankF1[0:64, 64:192], ut1[:, :], gsl, start=True, stop=True)))())
                    P.step("scalar", lambda e: e.activation(out=lamb[:], in_=bankF1[:, 0:64], func=AF.Exp))
                    P.step("scalar", lambda e: e.activation(out=lamT[:], in_=bankF1[0:64, 64:192], func=AF.Exp))
                    P.step("vector", lambda e: e.reciprocal(out=rlam[:], in_=lamb[:]))
                    P.step("vector", lambda e: e.tensor_scalar(out=rlam2[:], in0=rlam[:], scalar1=lamb[:, 63:64], scalar2=None, op0=ALU.mult))
                    P.step("vector", (lambda u=u: (lambda e: e.tensor_tensor(out=PTb[:], in0=wtf[:, u*64:(u+1)*64], in1=rlam[:], op=ALU.mult)))())
                    P.step("vector", (lambda u=u: (lambda e: e.tensor_tensor(out=PTpb[:], in0=wtf[:, u*64:(u+1)*64], in1=rlam2[:], op=ALU.mult)))())
                    P.step("tensor", lambda e: e.transpose(bankB1[0:64, 0:128], PTb[:], identb[:]))
                    P.step("vector", lambda e: e.tensor_copy(out=Pb2[:], in_=bankB1[0:64, 0:128]))
                    P.step("tensor", lambda e: e.transpose(bankB1[0:64, 128:256], PTpb[:], identb[:]))
                    P.step("vector", lambda e: e.tensor_copy(out=Ppb[:], in_=bankB1[0:64, 128:256]))
                    P.step("tensor", (lambda ksl=ksl: (lambda e: e.transpose(bankB1[0:64, 256:384], ksl, identb[:])))())
                    P.step("vector", (lambda bsl=bsl: (lambda e: e.tensor_scalar(out=Kb2[:], in0=bankB1[0:64, 256:384], scalar1=bsl, scalar2=None, op0=ALU.mult)))())
                    P.step("tensor", (lambda vsl=vsl: (lambda e: e.transpose(bankB1[0:64, 384:512], vsl, identb[:])))())
                    P.step("vector", (lambda bsl=bsl: (lambda e: e.tensor_scalar(out=Vb2[:], in0=bankB1[0:64, 384:512], scalar1=bsl, scalar2=None, op0=ALU.mult)))())
                    P.step("tensor", (lambda ksl=ksl, qsl=qsl: (lambda e: e.matmul(bankF1[0:64, 448:512], ksl, qsl, start=True, stop=True)))())
                    P.step("vector", (lambda bsl=bsl: (lambda e: e.scalar_tensor_tensor(out=ATb[:], in0=bankF1[0:64, 448:512], scalar=bsl, in1=mask64[:], op0=ALU.mult, op1=ALU.mult)))())
                    P.step("tensor", (lambda qsl=qsl, sk0_s=sk0_s: (lambda e: (
                        e.matmul(bankF1[0:64, 192:320], ATb[:], Pb2[:], start=True, stop=False),
                        e.matmul(bankF1[0:64, 192:320], qsl, sk0_s, start=False, stop=True))[1]))())
                    P.step("vector", lambda e: e.tensor_tensor(out=Ea[:], in0=bankF1[0:64, 192:320], in1=lamT[:], op=ALU.mult))
                    P.step("vector", lambda e: e.reduce_max(out=st[0:64, 0:1], in_=Ea[:], axis=AX.X))
                    P.step("vector", lambda e: e.tensor_scalar(out=st[0:64, 1:2], in0=st[0:64, 0:1], scalar1=-SCALE0, scalar2=None, op0=ALU.mult))
                    P.step("scalar", lambda e: e.activation(out=Ea[:], in_=Ea[:], func=AF.Exp, scale=SCALE0, bias=st[0:64, 1:2], accum_out=st[0:64, 2:3]))
                    P.step("vector", lambda e: e.reciprocal(out=st[0:64, 3:4], in_=st[0:64, 2:3]))
                    P.step("vector", lambda e: e.tensor_scalar(out=Ea[:], in0=Ea[:], scalar1=st[0:64, 3:4], scalar2=None, op0=ALU.mult))
                    P.step("vector", lambda e: e.tensor_tensor(out=ab2[:], in0=Ea[:], in1=lamT[:], op=ALU.mult))
                    P.step("tensor", lambda e: e.transpose(bankB2[:, 0:64], ab2[:], identb[0:64, 0:64]))
                    P.step("vector", lambda e: e.tensor_copy(out=aTb[:], in_=bankB2[:, 0:64]))
                    P.step("tensor", lambda e: e.matmul(bankF2[0:64, 384:448], PTb[:], aTb[:], start=True, stop=True))
                    P.step("vector", lambda e: e.tensor_tensor(out=AtTb[:], in0=bankF2[0:64, 384:448], in1=mask64[:], op=ALU.mult))
                    P.step("tensor", (lambda sv0_s=sv0_s: (lambda e: (
                        e.matmul(bankF1[0:64, 320:448], AtTb[:], Vb2[:], start=True, stop=False),
                        e.matmul(bankF1[0:64, 320:448], aTb[:], sv0_s, start=False, stop=True))[1]))())
                    P.step("vector", lambda e: e.tensor_copy(out=ot2[:], in_=bankF1[0:64, 320:448]))
                    P.step("scalar", lambda e: e.activation(out=Ea[:], in_=ot2[:], func=AF.Square))
                    P.step("vector", lambda e: e.reduce_sum(out=st[0:64, 4:5], in_=Ea[:], axis=AX.X))
                    P.step("vector", lambda e: e.tensor_scalar(out=st[0:64, 4:5], in0=st[0:64, 4:5], scalar1=1.0/128.0, scalar2=1e-5, op0=ALU.mult, op1=ALU.add))
                    P.step("vector", lambda e: e.reciprocal(out=st[0:64, 5:6], in_=st[0:64, 4:5]))
                    P.step("scalar", lambda e: e.activation(out=st[0:64, 5:6], in_=st[0:64, 5:6], func=AF.Sqrt))
                    P.step("vector", lambda e: e.tensor_scalar(out=ot2[:], in0=ot2[:], scalar1=st[0:64, 5:6], scalar2=None, op0=ALU.mult))
                    P.step("vector", lambda e: e.tensor_tensor(out=ot2[:], in0=ot2[:], in1=nwb[0:64, :], op=ALU.mult))
                    P.step("vector", (lambda sgsl=sgsl: (lambda e: e.tensor_copy(out=gtmp2[0:64, 0:128], in_=sgsl)))())
                    P.step("vector", (lambda u=u, hh=hh: (lambda e: e.tensor_tensor(
                        out=obuf2[0:64, u*1024 + hh*128 : u*1024 + (hh+1)*128], in0=ot2[:], in1=gtmp2[0:64, 0:128], op=ALU.mult)))())
                    P.step("tensor", lambda e: e.matmul(bankF2[:, 0:128], Ppb[:], Kb2[:], start=True, stop=True))
                    P.step("vector", (lambda sk_s=sk_s: (lambda e: e.scalar_tensor_tensor(
                        out=sk_s, in0=sk_s, scalar=lamb[:, 63:64], in1=bankF2[:, 0:128], op0=ALU.mult, op1=ALU.add)))())
                    P.step("tensor", lambda e: e.matmul(bankF2[:, 128:256], Ppb[:], Vb2[:], start=True, stop=True))
                    P.step("vector", (lambda sv_s=sv_s: (lambda e: e.scalar_tensor_tensor(
                        out=sv_s, in0=sv_s, scalar=lamb[:, 63:64], in1=bankF2[:, 128:256], op0=ALU.mult, op1=ALU.add)))())
                    P.step("tensor", (lambda sk_s=sk_s: (lambda e: e.transpose(bankF2[:, 256:384], sk_s, identf2[:])))())
                    P.step("vector", (lambda sk0_s=sk0_s: (lambda e: e.tensor_copy(out=sk0_s, in_=bankF2[:, 256:384])))())
                    P.step("vector", (lambda sv_s=sv_s, sv0_s=sv0_s: (lambda e: e.tensor_copy(out=sv0_s, in_=sv_s)))())
            P.step("gpsimd", (lambda tc=tc: (lambda e: e.dma_start(
                out=o_out[tc*512:(tc+1)*512, :].rearrange("(u p) c -> p u c", p=64),
                in_=obuf2.rearrange("p (u c) -> p u c", c=1024))))(), 16)

        P.emit(block, sem)
    return nc


def run_attn0(x2d, I):
    nc = _get("attn0", build_attn0)
    identb_np = np.eye(128, dtype=BF16NP)
    identf_np = np.eye(128, dtype=np.float32)
    ut1_np = np.triu(np.ones((64, 64))).astype(BF16NP)
    mask64_np = np.triu(np.ones((64, 64))).astype(np.float32)
    Wq = np.asarray(I["L0_attn_Wq"], dtype=np.float32)
    Wk = np.asarray(I["L0_attn_Wk"], dtype=np.float32)
    Wv = np.asarray(I["L0_attn_Wv"], dtype=np.float32)
    Ww = np.asarray(I["L0_attn_Ww"], dtype=np.float32)
    Wf2 = np.asarray(I["L0_attn_Wf2"], dtype=np.float32)
    Wb = np.asarray(I["L0_attn_Wb"], dtype=np.float32)
    Wg2 = np.asarray(I["L0_attn_Wg2"], dtype=np.float32)
    bg2 = np.asarray(I["L0_attn_bg2"], dtype=np.float32)
    cq = np.asarray(I["L0_attn_cq"], dtype=np.float32)
    ck = np.asarray(I["L0_attn_ck"], dtype=np.float32)
    cv = np.asarray(I["L0_attn_cv"], dtype=np.float32)
    def convlay(cm, sl):
        c = cm[sl]
        return np.ascontiguousarray(c.reshape(8, 128, 4).transpose(1, 0, 2).reshape(128, 32))
    in_maps = []
    for c in range(NCORES):
        b, g = c // 2, c % 2
        cs = slice(g*1024, (g+1)*1024)
        hs8 = slice(g*8, (g+1)*8)
        in_maps.append({
            "x_in": np.ascontiguousarray(x2d[b*T:(b+1)*T]),
            "wq_in": np.ascontiguousarray(Wq[:, cs]),
            "wk_in": np.ascontiguousarray(Wk[:, cs]),
            "wv_in": np.ascontiguousarray(Wv[:, cs]),
            "ww_in": np.ascontiguousarray(Ww[:, cs]),
            "wf1_in": np.ascontiguousarray(np.asarray(I["L0_attn_Wf1"], dtype=np.float32)),
            "wf2_in": np.ascontiguousarray(Wf2[:, cs]),
            "wb_in": np.ascontiguousarray(Wb[:, hs8]),
            "wg1_in": np.ascontiguousarray(np.asarray(I["L0_attn_Wg1"], dtype=np.float32)),
            "wg2_in": np.ascontiguousarray(Wg2[:, cs]),
            "bg2_in": np.ascontiguousarray(bg2[None, cs]),
            "cq_in": convlay(cq, cs), "ck_in": convlay(ck, cs), "cv_in": convlay(cv, cs),
            "normw_in": np.asarray(I["L0_attn_norm_w"], dtype=np.float32)[None, :],
            "ut1_in": ut1_np, "mask64_in": mask64_np, "ident_in": identb_np,
            "identf_in": identf_np,
        })
    res = run_bass_kernel_spmd(nc, in_maps, core_ids=list(range(NCORES)))
    o = np.zeros((TOK, 2048), dtype=BF16NP)
    for c in range(NCORES):
        b, g = c // 2, c % 2
        o[b*T:(b+1)*T, g*1024:(g+1)*1024] = res.results[c]["o_out"]
    if DEBUG_ATTN0:
        return o, res
    return o


_BUILD_CACHE = {}

def _get(name, builder):
    if name not in _BUILD_CACHE:
        _BUILD_CACHE[name] = builder()
    return _BUILD_CACHE[name]


def run_ffn(kind, x2d, dq, oT=None, wo=None):
    L0 = (kind == "L0")
    od = 0 if oT is None else oT.shape[0]
    nc = _get(f"ffn_{kind}_{od}", lambda: build_ffn(kind, od))
    identb_np = np.eye(128, dtype=BF16NP)
    identf_np = np.eye(128, dtype=np.float32)
    in_maps = []
    for c in range(NCORES):
        m = {
            "x_in": np.ascontiguousarray(x2d[c*TC:(c+1)*TC]),
            "ident_in": identb_np,
            "identf_in": identf_np,
            "wgb": dq["wg" if L0 else "g1"],
            "wub": dq["wu" if L0 else "u1"],
            "wdb": dq["wd" if L0 else "dn1"],
        }
        if od:
            m["oT_in"] = np.ascontiguousarray(oT[:, c*TC:(c+1)*TC])
            m["wo_in"] = wo
        in_maps.append(m)
    res = run_bass_kernel_spmd(nc, in_maps, core_ids=list(range(NCORES)))
    return np.concatenate([res.results[c]["x_out"] for c in range(NCORES)], axis=0)


def kernel(**inputs):
    I = {k: np.asarray(v) for k, v in inputs.items()}
    x = I["L0_attn_hidden_states"].astype(np.float32)
    x2d = np.ascontiguousarray(x.reshape(TOK, D))

    nc0 = _get("dequant", build_dequant)
    res0 = run_bass_kernel_spmd(nc0, [dequant_inputs(c, I) for c in range(NCORES)],
                                core_ids=list(range(NCORES)))
    dq = assemble_dequant([res0.results[c] for c in range(NCORES)])

    x2d = run_ffn("L0", x2d, dq)
    o0 = run_attn0(x2d, I)
    oT0 = np.ascontiguousarray(o0.T)
    x2d = run_ffn("L0", x2d, dq, oT=oT0, wo=dq["wo0"])
    x2d = run_ffn("L1", x2d, dq)
    oT1 = run_attn1(x2d, I)
    x2d = run_ffn("L1", x2d, dq, oT=oT1, wo=dq["wo1"])
    return np.ascontiguousarray(x2d.reshape(B, T, D)).astype(np.float32)
